# version 04
# baseline (speedup 1.0000x reference)
import numpy as np
import concourse.bass as bass
import concourse.mybir as mybir
from concourse.bass_utils import run_bass_kernel_spmd
from contextlib import ExitStack

F32 = mybir.dt.float32
BF16 = mybir.dt.bfloat16
ALU = mybir.AluOpType
AF = mybir.ActivationFunctionType
AX = mybir.AxisListType.X

SEQ = 4096
DM = 1024
DD = 256
NT = SEQ // 128
NSTEP = 8
DBG = False
SCEMIT = frozenset(["mma", "st1a", "st1", "rstd1", "mask", "chain",
                    "mm2", "spn", "act2", "st2d", "rstd2", "xhb"])
NEUTER = frozenset()
INV128 = 1.0 / 128.0
EPS6 = 1e-6
EPS5 = 1e-5
BIG = 1e30


class Sem:
    def __init__(self, ctx, nc, name):
        self.h = ctx.enter_context(nc.semaphore(name))
        self.v = 0

    def inc(self, inst, k=1):
        inst.then_inc(self.h, k)
        self.v += k


def bc(col, n):
    return col.broadcast_to([col.shape[0], n])


def tpos(row_base, col_base):
    if row_base == 0 and col_base == 0:
        return None
    return (row_base, col_base)


def MM(te, out, lhsT, rhs, **kw):
    kw.setdefault("skip_group_check", True)
    return te.matmul(out, lhsT, rhs, **kw)


def build_nc():
    nc = bass.Bass("TRN2", target_bir_lowering=False)
    d = {}

    def din(name, shape, dt=F32):
        d[name] = nc.dram_tensor(name, shape, dt, kind="ExternalInput").ap()

    din("x", [2, SEQ, DM])
    din("down_w", [DM, DD]); din("down_b", [1, DD])
    din("up_w", [DD, DM]); din("up_b", [1, DM])
    din("wq", [DD, DD]); din("wk", [DD, DD]); din("wv", [DD, DD]); din("wo", [DD, DD])
    din("w1init", [128, 256])
    din("b1init", [4, 128])
    din("lr_wT", [DD, 2]); din("lr_b", [1, 2])
    din("tokrep", [128, 16])
    din("invkrep", [128, 16])
    din("post_w_c", [128, 2]); din("post_b_c", [128, 2])
    din("ln_w_c", [128, 2]); din("ln_b_c", [128, 2])
    din("gate_wT", [DM, 8]); din("gate_b", [1, 8])
    din("ew_t", [8, DM, DM]); din("expert_b", [8, DM])
    din("gw_rep", [128, 256])
    din("b_rep", [128, 256])
    din("gscan0", [128, 128])
    din("identity_bf", [128, 128], BF16)
    din("ones_row_bf", [1, 128], BF16)
    din("ind_bf", [4, 128], BF16)
    din("ind2_bf", [128, 4], BF16)
    din("smask", [128, 64])
    y = nc.dram_tensor("y", [2, SEQ, DM], F32, kind="ExternalOutput").ap()

    ctx = ExitStack()
    with ctx:
        def sb(name, shape, dt=F32):
            return ctx.enter_context(nc.sbuf_tensor(name, shape, dt))

        def psf(name):
            return ctx.enter_context(nc.psum_tensor(name, [128, 512], F32))

        pw1 = psf("pw1")
        pzc = psf("pzc")
        psp = psf("psp")
        pe0 = psf("pe0")
        pe1 = psf("pe1")
        pu0 = psf("pu0")
        pu1 = psf("pu1")
        put = psf("put")
        put_bf = put[:].bitcast(BF16)
        pe1_bf = pe1[:].bitcast(BF16)

        ewb = sb("ewb", [128, 65536], BF16)
        dwb = sb("dwb", [128, 2048], BF16)
        upb = sb("upb", [128, 2048], BF16)
        wqb = sb("wqb", [128, 512], BF16)
        wkb = sb("wkb", [128, 512], BF16)
        wvb = sb("wvb", [128, 512], BF16)
        wob = sb("wob", [128, 512], BF16)
        gwb = sb("gwb", [128, 64], BF16)
        lrwb = sb("lrwb", [128, 4], BF16)
        expb = sb("expb", [8, 1024], BF16)
        rowsb = sb("rowsb", [1, 266], BF16)
        rowbpw = sb("rowbpw", [1, 256], BF16)
        rowup = sb("rowup", [1, 1024], BF16)
        w8tb = sb("w8tb", [8, 128], BF16)
        w8bf = [sb(f"w8bf{s}", [128, 8], BF16) for s in range(2)]
        idb = sb("idb", [128, 128], BF16)
        ones1 = sb("ones1", [1, 128], BF16)
        indb = sb("indb", [4, 128], BF16)
        ind2b = sb("ind2b", [128, 4], BF16)
        zb4 = sb("zb4", [128, 4], BF16)
        masktok = sb("masktok_s", [128, 64])
        g2scan = sb("g2scan_s", [128, 128])
        gw_rep = sb("gw_rep_s", [128, 256])
        b_rep = sb("b_rep_s", [128, 256])
        ccol = sb("ccol", [128, 4])
        pwcol = sb("pwcol", [128, 2])
        lwcol = sb("lwcol", [128, 2])
        pbcolb = sb("pbcolb", [128, 2], BF16)
        lbcolb = sb("lbcolb", [128, 2], BF16)

        w1s = sb("w1s", [128, 512], BF16)
        rsw1 = sb("rsw1", [128, 4])
        rsw1b = sb("rsw1b", [128, 4], BF16)
        b1c = sb("b1c", [4, 128])
        b1rs = sb("b1rs", [4, 1])
        b1eb = sb("b1eb", [4, 132], BF16)

        xb = [[sb(f"xb{s}{i}", [128, DM]) for i in range(2)] for s in range(2)]
        xnb = [sb(f"xnb{s}", [128, DM], BF16) for s in range(2)]
        xnf = [sb(f"xnf{s}", [128, DM], BF16) for s in range(2)]
        htb = [sb(f"htb{s}", [128, DD], BF16) for s in range(2)]
        hfm = [sb(f"hfm{s}", [128, DD], BF16) for s in range(2)]
        xqt = [sb(f"xqt{s}", [128, DD]) for s in range(2)]
        xkt = [sb(f"xkt{s}", [128, DD]) for s in range(2)]
        xvt = [sb(f"xvt{s}", [128, DD]) for s in range(2)]
        xqbt = [sb(f"xqbt{s}", [128, DD], BF16) for s in range(2)]
        xkbt = [sb(f"xkbt{s}", [128, DD], BF16) for s in range(2)]
        kqt = [sb(f"kqt{s}", [128, 512], BF16) for s in range(2)]
        sgm = [sb(f"sgm{s}", [128, 4]) for s in range(2)]
        gbtt = [sb(f"gbtt{s}", [128, DD]) for s in range(2)]
        xks = sb("xks", [128, NSTEP * 128], BF16)
        gbts = sb("gbts", [128, NSTEP * 128])
        sgs = sb("sgs", [128, 16])
        xhbs = sb("xhbs", [128, NSTEP * 128])
        s4t = sb("s4t", [128, 128], BF16)
        gsn = sb("gsn", [128, 128], BF16)
        tsg = sb("tsg", [128, 128], BF16)
        sw = [sb(f"sw{i}", [128, 128]) for i in range(5)]
        scols = sb("scols", [128, 16])
        c12 = sb("c12", [128, 2])
        xhbt = [sb(f"xhbt{s}", [128, DD]) for s in range(2)]
        ows = [sb(f"ows{s}", [128, DD]) for s in range(2)]
        glog = [sb(f"glog{s}", [128, 8]) for s in range(2)]
        gsc = [sb(f"gsc{s}", [128, 32]) for s in range(2)]
        w8 = [sb(f"w8{s}", [128, 8]) for s in range(2)]
        lnc = sb("lnc", [128, 48])

        S = {}
        for name in ["ldx", "ldw", "ldw2", "ldsm", "stp", "te", "dve", "act", "gp", "resh",
                     "sto", "z1", "gsn", "z2m", "mir", "var1", "std1",
                     "var2", "std2", "st2", "st1a", "msk", "sp", "xhb", "evA", "dbg",
                     "sx1", "c12s", "qrr", "ve",
                     "evB", "phU", "comb", "expmm", "w8s", "xus"]:
            S[name] = Sem(ctx, nc, "s_" + name)

        block = ctx.enter_context(nc.Block())

        jobs = []
        for k in range(8):
            for e in range(8):
                jobs.append((d["ew_t"][e, 128 * k:128 * (k + 1), :], 1024,
                             ewb[:, (k * 8 + e) * 1024:(k * 8 + e + 1) * 1024]))
        for c in range(8):
            jobs.append((d["down_w"][128 * c:128 * (c + 1), :], 256,
                         dwb[:, 256 * c:256 * (c + 1)]))
        for c in range(2):
            jobs.append((d["up_w"][128 * c:128 * (c + 1), :], 1024,
                         upb[:, 1024 * c:1024 * (c + 1)]))
        for nm, dst in [("wq", wqb), ("wk", wkb), ("wv", wvb), ("wo", wob)]:
            for c in range(2):
                jobs.append((d[nm][128 * c:128 * (c + 1), :], 256,
                             dst[:, 256 * c:256 * (c + 1)]))
        for c in range(8):
            jobs.append((d["gate_wT"][128 * c:128 * (c + 1), :], 8,
                         gwb[:, 8 * c:8 * (c + 1)]))
        for c in range(2):
            jobs.append((d["lr_wT"][128 * c:128 * (c + 1), :], 2,
                         lrwb[:, 2 * c:2 * (c + 1)]))

        for i, (src, ncols, dst) in enumerate(jobs):
            stg = xb[0][i % 2]
            lsem = S["ldw"] if i % 2 == 0 else S["ldw2"]

            def sec_load(sync, src=src, ncols=ncols, stg=stg, i=i, lsem=lsem):
                if i >= 2:
                    sync.wait_ge(S["stp"].h, i - 1)
                lsem.inc(sync.dma_start(stg[:, 0:ncols], src), 16)
            block.sync(sec_load)

            def sec_cast(vector, ncols=ncols, dst=dst, stg=stg, lsem=lsem):
                vector.wait_ge(lsem.h, lsem.v)
                S["stp"].inc(vector.tensor_copy(dst, stg[:, 0:ncols]))
            block.vector(sec_cast)
        njobs = len(jobs)

        def sec_small(sync):
            for dst, src in [
                (idb[:], d["identity_bf"]), (ones1[:], d["ones_row_bf"]),
                (indb[:], d["ind_bf"]), (ind2b[:], d["ind2_bf"]),
                (masktok[:], d["smask"]),
                (pwcol[:], d["post_w_c"]), (lwcol[:], d["ln_w_c"]),
                (ccol[:, 0:2] if False else sw[2][:, 0:2], d["post_b_c"]),
                (sw[2][:, 2:4], d["ln_b_c"]),
                (b1c[:], d["b1init"]),
                (xhbs[32:33, 0:1024], d["up_b"]),
                (gw_rep[:], d["gw_rep"]), (b_rep[:], d["b_rep"]),
                (g2scan[:], d["gscan0"]),
                (sw[2][:, 4:20], d["tokrep"]), (sw[3][:, 0:16], d["invkrep"]),
                (gsc[0][0:1, 0:8], d["gate_b"]),
                (gsc[1][0:1, 0:2], d["lr_b"]),
                (gbtt[0][0:1, 0:256], d["down_b"]),
                (xhbs[0:8, 0:1024], d["expert_b"]),
                (gbtt[1][0:128, 0:128], d["w1init"][:, 0:128]),
                (xhbt[0][0:128, 0:128], d["w1init"][:, 128:256]),
            ]:
                S["ldsm"].inc(sync.dma_start(dst, src), 16)
        block.sync(sec_small)

        def sec_set1(vector):
            vector.wait_ge(S["ldsm"].h, S["ldsm"].v)
            vector.wait_ge(S["stp"].h, njobs)
            vector.memset(s4t[:], 0.0)
            vector.memset(zb4[:], 0.0)
            vector.memset(psp[:, 0:64], 0.0)
            vector.memset(gbts[:], 0.0)
            vector.memset(sgs[:], 0.0)
            vector.memset(xks[:], 0.0)
            vector.memset(ccol[:, 1:2], EPS6)
            vector.memset(ccol[:, 2:3], EPS5)
            vector.tensor_copy(rowsb[0:1, 0:256], gbtt[0][0:1, 0:256])
            vector.tensor_copy(rowsb[0:1, 256:264], gsc[0][0:1, 0:8])
            vector.tensor_copy(rowsb[0:1, 264:266], gsc[1][0:1, 0:2])
            vector.tensor_copy(expb[:], xhbs[0:8, 0:1024])
            vector.tensor_copy(pbcolb[:], sw[2][:, 0:2])
            vector.tensor_copy(lbcolb[:], sw[2][:, 2:4])
            vector.tensor_copy(b1eb[:, 0:128], b1c[:])
            _r = vector.reduce_sum(b1rs[:], b1c[:], axis=AX)
            S["ve"].inc(_r)
            vector.wait_ge(S["ve"].h, S["ve"].v)
            _li = vector.tensor_copy(b1eb[:, 128:129], b1rs[:])
            S["stp"].inc(_li)
        block.vector(sec_set1)

        def sec_set1gp(gpsimd):
            gpsimd.wait_ge(S["ldsm"].h, S["ldsm"].v)
            gpsimd.wait_ge(S["stp"].h, njobs)
            tkc = sw[2][:, 4:20]
            gpsimd.tensor_add(tkc, tkc, sw[3][:, 0:16])
            gpsimd.tensor_relu(tkc, tkc)
            gpsimd.tensor_single_scalar(tkc, tkc, INV128, ALU.mult)
            gpsimd.tensor_single_scalar(ccol[:, 0:1], sw[2][:, 19:20], -1.0,
                                        ALU.mult)
            _g = gpsimd.tensor_single_scalar(sw[3][:, 0:16], tkc, -1.0, ALU.mult)
            S["stp"].inc(_g)
        block.gpsimd(sec_set1gp)

        def sec_set1b(vector):
            vector.wait_ge(S["stp"].h, njobs + 2)
            vector.tensor_tensor(
                masktok[:].rearrange("p (b i) -> p b i", b=4),
                masktok[:].rearrange("p (b i) -> p b i", b=4),
                bass.AP(sw[3], 0, [[128, 128], [0, 4], [1, 16]]), ALU.mult)
            _g = vector.tensor_mul(g2scan[:], g2scan[:], g2scan[:])
            S["stp"].inc(_g)
        block.vector(sec_set1b)

        def sec_gp_id(gpsimd):
            gpsimd.wait_ge(S["ldsm"].h, S["ldsm"].v)
            gpsimd.wait_ge(S["stp"].h, njobs)
            gpsimd.memset(sw[1][:], 1.0)
            S["gp"].inc(gpsimd.affine_select(
                sw[1][:], sw[1][:], pattern=[[-1, 128]], compare_op=ALU.is_equal,
                fill=0.0, base=0, channel_multiplier=1))
        block.gpsimd(sec_gp_id)

        def sec_te_w1(tensor):
            tensor.wait_ge(S["gp"].h, S["gp"].v)
            w1i = []
            for g in range(4):
                src = gbtt[1] if (g % 2) == 0 else xhbt[0]
                w1i.append(MM(tensor, pw1[:, 128 * g:128 * (g + 1)], sw[1][:],
                              src[:, 0:128], start=(g == 0), stop=(g == 3)))
            S["te"].inc(w1i[-1])
        block.tensor(sec_te_w1)

        def sec_act_pin(scalar):
            scalar.add_instruction(mybir.InstLoadActFuncSet(
                name=nc.get_next_instruction_name(), act_func_set_id=6,
                ins=[], outs=[]))
        block.scalar(sec_act_pin)

        def sec_act_mir0(scalar):
            scalar.wait_ge(S["te"].h, S["te"].v)
            for g in range(4):
                scalar.activation(w1s[:, 128 * g:128 * (g + 1)],
                                  pw1[:, 128 * g:128 * (g + 1)], AF.Copy,
                                  accum_out=rsw1[:, g:g + 1])
            _li = scalar.activation(rsw1b[:], rsw1[:], AF.Copy)
            S["mir"].inc(_li)
        block.scalar(sec_act_mir0)

        def sec_te_rows(tensor):
            tensor.wait_ge(S["stp"].h, njobs + 3)
            for c in range(2):
                MM(tensor, put[0:1, 0:256], pbcolb[:, c:c + 1],
                              wob[:, 256 * c:256 * (c + 1)],
                              start=(c == 0), stop=(c == 1))
            rowi = []
            for c in range(2):
                for hh in range(2):
                    pu = pu0 if hh == 0 else pu1
                    rowi.append(MM(tensor, pu[0:1, 0:512],
                                  lbcolb[:, c:c + 1],
                                  upb[:, 1024 * c + 512 * hh:1024 * c + 512 * (hh + 1)],
                                  start=(c == 0), stop=(c == 1)))
            S["te"].inc(rowi[-1])
        block.tensor(sec_te_rows)

        def sec_dve_rows(vector):
            vector.wait_ge(S["te"].h, S["te"].v)
            vector.tensor_copy(rowbpw[0:1, 0:256], put[0:1, 0:256])
            vector.tensor_add(xhbs[32:33, 0:512], xhbs[32:33, 0:512], pu0[0:1, 0:512])
            vector.tensor_add(xhbs[32:33, 512:1024], xhbs[32:33, 512:1024], pu1[0:1, 0:512])
            vector.tensor_copy(rowup[0:1, 0:1024], xhbs[32:33, 0:1024])
            vector.memset(xhbs[:], 0.0)
            for c in range(2):
                vector.tensor_single_scalar(wob[:, 256 * c:256 * (c + 1)],
                                            wob[:, 256 * c:256 * (c + 1)],
                                            pwcol[:, c:c + 1], ALU.mult)
                _li = vector.tensor_single_scalar(upb[:, 1024 * c:1024 * (c + 1)],
                                            upb[:, 1024 * c:1024 * (c + 1)],
                                            lwcol[:, c:c + 1], ALU.mult)
            S["stp"].inc(_li)
        block.vector(sec_dve_rows)
        STP_DONE = S["stp"].v

        def sec_x01(sync):
            sync.wait_ge(S["stp"].h, njobs)
            for it in range(min(2, NT)):
                for s in range(2):
                    S["ldx"].inc(sync.dma_start(
                        xb[s][it % 2][:], d["x"][s, 128 * it:128 * (it + 1), :]), 16)
        block.sync(sec_x01)

        for it in range(NT):
            buf = it % 2

            for s in range(2):
                def ph_ln1_act(scalar, s=s, it=it, buf=buf):
                    scalar.wait_ge(S["ldx"].h, 32 * (it + 1))
                    scalar.wait_ge(S["sto"].h, S["sto"].v)
                    scalar.wait_ge(S["evA"].h, S["evA"].v)
                    scalar.wait_ge(S["expmm"].h, S["expmm"].v)
                    if it == 0:
                        scalar.wait_ge(S["stp"].h, STP_DONE)
                    scalar.activation(xnf[s][:], xb[s][buf][:], AF.Copy,
                                      accum_out=lnc[:, 8 * s:8 * s + 1])
                    _li = scalar.activation(xnf[s][:], xb[s][buf][:], AF.Square,
                                      accum_out=lnc[:, 8 * s + 1:8 * s + 2])
                    S["act"].inc(_li)
                block.scalar(ph_ln1_act)

                def ph_ln1_gp2(gpsimd, s=s):
                    gpsimd.wait_ge(S["act"].h, S["act"].v)
                    b0 = 8 * s
                    gpsimd.tensor_single_scalar(lnc[:, b0 + 2:b0 + 3],
                                                lnc[:, b0:b0 + 1], 1.0 / DM, ALU.mult)
                    gpsimd.tensor_single_scalar(lnc[:, b0 + 3:b0 + 4],
                                                lnc[:, b0 + 1:b0 + 2], 1.0 / DM, ALU.mult)
                    gpsimd.tensor_mul(lnc[:, b0 + 4:b0 + 5], lnc[:, b0 + 2:b0 + 3],
                                      lnc[:, b0 + 2:b0 + 3])
                    _li = gpsimd.tensor_sub(lnc[:, b0 + 5:b0 + 6], lnc[:, b0 + 3:b0 + 4],
                                      lnc[:, b0 + 4:b0 + 5])
                    S["gp"].inc(_li)
                block.gpsimd(ph_ln1_gp2)

                def ph_ln1_act2(scalar, s=s):
                    scalar.wait_ge(S["gp"].h, S["gp"].v)
                    b0 = 8 * s
                    scalar.activation(lnc[:, b0 + 5:b0 + 6], lnc[:, b0 + 5:b0 + 6],
                                      AF.Ln, bias=ccol[:, 2:3])
                    _li = scalar.activation(lnc[:, b0 + 6:b0 + 7], lnc[:, b0 + 5:b0 + 6],
                                      AF.Exp, scale=-0.5)
                    S["act"].inc(_li)
                block.scalar(ph_ln1_act2)

                def ph_ln1_dve(vector, s=s, buf=buf):
                    vector.wait_ge(S["act"].h, S["act"].v)
                    b0 = 8 * s
                    _li = vector.tensor_scalar(xnb[s][:], xb[s][buf][:],
                                         lnc[:, b0 + 2:b0 + 3], lnc[:, b0 + 6:b0 + 7],
                                         ALU.subtract, ALU.mult)
                    S["dve"].inc(_li)
                block.vector(ph_ln1_dve)

            for s in range(2):
                for grp in range(2):
                    pbank = put_bf if grp == 0 else pe1_bf

                    def ph_txn_te(tensor, s=s, grp=grp, pbank=pbank):
                        tensor.wait_ge(S["dve"].h, S["dve"].v)
                        tensor.wait_ge(S["comb"].h, S["comb"].v)
                        tensor.wait_ge(S["act"].h, S["act"].v)
                        tensor.wait_ge(S["evA"].h, S["evA"].v)
                        ti = []
                        for j in range(4):
                            c = grp * 4 + j
                            ti.append(tensor.transpose(
                                pbank[:, 128 * j:128 * (j + 1)],
                                xnb[s][:, 128 * c:128 * (c + 1)], idb[:]))
                        S["te"].inc(ti[-1])
                    block.tensor(ph_txn_te)

                    def ph_txn_ev(vector, s=s, grp=grp, pbank=pbank):
                        vector.wait_ge(S["te"].h, S["te"].v)
                        vector.wait_ge(S["expmm"].h, S["expmm"].v)
                        for j in range(4):
                            c = grp * 4 + j
                            _li = vector.tensor_copy(xnf[s][:, 128 * c:128 * (c + 1)],
                                               pbank[:, 128 * j:128 * (j + 1)])
                        S["evA"].inc(_li)
                    block.vector(ph_txn_ev)

            for s in range(2):
                def ph_down_te(tensor, s=s):
                    tensor.wait_ge(S["evA"].h, S["evA"].v)
                    tensor.wait_ge(S["comb"].h, S["comb"].v)
                    tensor.wait_ge(S["act"].h, S["act"].v)
                    tensor.wait_ge(S["dve"].h, S["dve"].v)
                    for c in range(8):
                        MM(tensor, put[:, 0:256], xnf[s][:, 128 * c:128 * (c + 1)],
                                      dwb[:, 256 * c:256 * (c + 1)],
                                      start=(c == 0), stop=False)
                    MM(tensor, put[:, 0:256], ones1[:], rowsb[0:1, 0:256],
                                  start=False, stop=True)
                    for c in range(8):
                        MM(tensor, pe0[:, 256:264], xnf[s][:, 128 * c:128 * (c + 1)],
                                      gwb[:, 8 * c:8 * (c + 1)],
                                      start=(c == 0), stop=False)
                    _li = MM(tensor, pe0[:, 256:264], ones1[:], rowsb[0:1, 256:264],
                                  start=False, stop=True)
                    S["te"].inc(_li)
                block.tensor(ph_down_te)

                def ph_down_act(scalar, s=s):
                    scalar.wait_ge(S["te"].h, S["te"].v)
                    _li = scalar.activation(xvt[s][:], put[:, 0:256], AF.Exp, scale=-1.0)
                    S["act"].inc(_li)
                block.scalar(ph_down_act)

                def ph_down_dve(vector, s=s):
                    vector.wait_ge(S["act"].h, S["act"].v)
                    vector.wait_ge(S["w8s"].h, S["w8s"].v)
                    vector.tensor_single_scalar(xvt[s][:], xvt[s][:], 1.0, ALU.add)
                    vector.reciprocal(xqt[s][:], xvt[s][:])
                    vector.tensor_tensor(htb[s][:], put[:, 0:256], xqt[s][:], ALU.mult)
                    _g = vector.tensor_copy(glog[s][:], pe0[:, 256:264])
                    S["ve"].inc(_g)
                    _li = vector.tensor_copy(htb[s][:, 0:8], htb[s][:, 0:8])
                    S["dve"].inc(_li)
                block.vector(ph_down_dve)

                def ph_gate_dve1(vector, s=s):
                    vector.wait_ge(S["dve"].h, S["dve"].v)
                    vector.wait_ge(S["ve"].h, S["ve"].v)
                    g = glog[s]; sc = gsc[s]
                    _e = vector.reduce_max(sc[:, 0:1], g[:], axis=AX)
                    S["ve"].inc(_e)
                    vector.wait_ge(S["ve"].h, S["ve"].v)
                    _d = vector.tensor_single_scalar(sc[:, 4:5], sc[:, 0:1],
                                                     -1.0, ALU.mult)
                    _e = vector.tensor_tensor(sc[:, 24:32], g[:], bc(sc[:, 0:1], 8),
                                              ALU.is_ge)
                    S["ve"].inc(_e)
                    vector.wait_ge(S["ve"].h, S["ve"].v)
                    _e = vector.scalar_tensor_tensor(sc[:, 16:24], sc[:, 24:32],
                                                     -BIG, g[:], ALU.mult, ALU.add)
                    S["ve"].inc(_e)
                    vector.wait_ge(S["ve"].h, S["ve"].v)
                    _e = vector.reduce_max(sc[:, 3:4], sc[:, 16:24], axis=AX)
                    S["ve"].inc(_e)
                    vector.wait_ge(S["ve"].h, S["ve"].v)
                    _li = vector.tensor_tensor(sc[:, 20:24] if False else sc[:, 24:32],
                                               g[:], bc(sc[:, 3:4], 8),
                                               ALU.is_ge)
                    S["dve"].inc(_li)
                block.vector(ph_gate_dve1)

                def ph_gate_act(scalar, s=s):
                    scalar.wait_ge(S["dve"].h, S["dve"].v)
                    _li = scalar.activation(gsc[s][:, 8:16], glog[s][:], AF.Exp,
                                            bias=gsc[s][:, 4:5],
                                            accum_out=gsc[s][:, 1:2])
                    S["act"].inc(_li)
                block.scalar(ph_gate_act)

                def ph_gate_dve2(vector, s=s):
                    vector.wait_ge(S["act"].h, S["act"].v)
                    vector.wait_ge(S["comb"].h, S["comb"].v)
                    sc = gsc[s]
                    _e = vector.reciprocal(sc[:, 2:3], sc[:, 1:2])
                    S["ve"].inc(_e)
                    _e = vector.tensor_mul(sc[:, 8:16], sc[:, 8:16], sc[:, 24:32])
                    S["ve"].inc(_e)
                    vector.wait_ge(S["ve"].h, S["ve"].v)
                    _e = vector.tensor_tensor(w8[s][:], sc[:, 8:16],
                                              bc(sc[:, 2:3], 8), ALU.mult)
                    S["ve"].inc(_e)
                    vector.wait_ge(S["ve"].h, S["ve"].v)
                    _li = vector.tensor_copy(w8bf[s][:], w8[s][:])
                    S["w8s"].inc(_li)
                block.vector(ph_gate_dve2)

            for s in range(2):
                def ph_th_te(tensor, s=s):
                    tensor.wait_ge(S["dve"].h, S["dve"].v)
                    ti = []
                    for c in range(2):
                        ti.append(tensor.transpose(
                            pe1_bf[:, 128 * c:128 * (c + 1)],
                            htb[s][:, 128 * c:128 * (c + 1)], idb[:]))
                    S["te"].inc(ti[-1])
                block.tensor(ph_th_te)

                def ph_th_ev(vector, s=s):
                    vector.wait_ge(S["te"].h, S["te"].v)
                    for c in range(2):
                        _li = vector.tensor_copy(hfm[s][:, 128 * c:128 * (c + 1)],
                                           pe1_bf[:, 128 * c:128 * (c + 1)])
                    S["evB"].inc(_li)
                block.vector(ph_th_ev)

                def ph_qkv_te(tensor, s=s):
                    tensor.wait_ge(S["evB"].h, S["evB"].v)
                    tensor.wait_ge(S["act"].h, S["act"].v)
                    tensor.wait_ge(S["dve"].h, S["dve"].v)
                    for c in range(2):
                        MM(tensor, put[:, 0:256], hfm[s][:, 128 * c:128 * (c + 1)],
                                      wqb[:, 256 * c:256 * (c + 1)],
                                      start=(c == 0), stop=(c == 1))
                        MM(tensor, put[:, 256:512], hfm[s][:, 128 * c:128 * (c + 1)],
                                      wkb[:, 256 * c:256 * (c + 1)],
                                      start=False, stop=False)
                        MM(tensor, pe0[:, 0:256], hfm[s][:, 128 * c:128 * (c + 1)],
                                      wvb[:, 256 * c:256 * (c + 1)],
                                      start=(c == 0), stop=False)
                        MM(tensor, pe0[:, 448:450], hfm[s][:, 128 * c:128 * (c + 1)],
                                      lrwb[:, 2 * c:2 * (c + 1)], start=False, stop=False)
                    _li = MM(tensor, pe0[:, 448:450], ones1[:], rowsb[0:1, 264:266],
                                  start=False, stop=True)
                    S["te"].inc(_li)
                block.tensor(ph_qkv_te)

                def ph_qkv_ev(vector, s=s):
                    vector.wait_ge(S["te"].h, S["te"].v)
                    vector.wait_ge(S["resh"].h, S["resh"].v)
                    vector.tensor_copy(xqt[s][:], put[:, 0:256])
                    vector.tensor_copy(xkt[s][:], put[:, 256:512])
                    vector.tensor_copy(xvt[s][:], pe0[:, 0:256])
                    vector.tensor_copy(xqbt[s][:], put[:, 0:256])
                    _li = vector.tensor_copy(xkbt[s][:], put[:, 256:512])
                    S["dve"].inc(_li)
                block.vector(ph_qkv_ev)

                def ph_qkv_act(scalar, s=s):
                    scalar.wait_ge(S["te"].h, S["te"].v)
                    scalar.wait_ge(S["resh"].h, S["resh"].v)
                    _li = scalar.activation(sgm[s][:, 2:4], pe0[:, 448:450], AF.Exp,
                                      scale=-1.0)
                    S["act"].inc(_li)
                block.scalar(ph_qkv_act)

                def ph_sgm_gp(gpsimd, s=s):
                    gpsimd.wait_ge(S["act"].h, S["act"].v)
                    _li = gpsimd.tensor_single_scalar(sgm[s][:, 2:4],
                                                      sgm[s][:, 2:4], 1.0, ALU.add)
                    S["gp"].inc(_li)
                block.gpsimd(ph_sgm_gp)

                def ph_sgm_dve(vector, s=s):
                    vector.wait_ge(S["gp"].h, S["gp"].v)
                    _li = vector.reciprocal(sgm[s][:, 0:2], sgm[s][:, 2:4])
                    S["dve"].inc(_li)
                block.vector(ph_sgm_dve)

                def ph_tkq_te(tensor, s=s):
                    tensor.wait_ge(S["dve"].h, S["dve"].v)
                    ti = []
                    for hh in range(2):
                        ti.append(tensor.transpose(
                            pe1_bf[:, 128 * hh:128 * (hh + 1)],
                            xkbt[s][:, 128 * hh:128 * (hh + 1)], idb[:]))
                    for hh in range(2):
                        ti.append(tensor.transpose(
                            pe1_bf[:, 256 + 128 * hh:256 + 128 * (hh + 1)],
                            xqbt[s][:, 128 * hh:128 * (hh + 1)], idb[:]))
                    S["te"].inc(ti[-1])
                block.tensor(ph_tkq_te)

                def ph_tkq_ev(vector, s=s):
                    vector.wait_ge(S["te"].h, S["te"].v)
                    vector.wait_ge(S["z1"].h, S["z1"].v)
                    vector.wait_ge(S["sp"].h, S["sp"].v)
                    vector.wait_ge(S["phU"].h, S["phU"].v)
                    for hh in range(2):
                        dstk = kqt[s][:, 256 * hh:256 * (hh + 1)].rearrange(
                            "p (n kq j) -> p n kq j", n=8, kq=2)[:, :, 0, :]
                        dstq = kqt[s][:, 256 * hh:256 * (hh + 1)].rearrange(
                            "p (n kq j) -> p n kq j", n=8, kq=2)[:, :, 1, :]
                        vector.tensor_copy(
                            dstk, pe1_bf[:, 128 * hh:128 * (hh + 1)].rearrange(
                                "p (n j) -> p n j", n=8))
                        _li = vector.tensor_copy(
                            dstq, pe1_bf[:, 256 + 128 * hh:256 + 128 * (hh + 1)].rearrange(
                                "p (n j) -> p n j", n=8))
                    S["evB"].inc(_li)
                block.vector(ph_tkq_ev)

                def ph_gbt(vector, s=s):
                    vector.wait_ge(S["resh"].h, S["resh"].v)
                    vector.tensor_sub(ows[s][:], xvt[s][:], xkt[s][:])
                    vector.tensor_sub(ows[s][:], b_rep[:], ows[s][:])
                    _li = vector.tensor_mul(gbtt[s][:], ows[s][:], gw_rep[:])
                    S["dve"].inc(_li)
                block.vector(ph_gbt)

            def mk_resh(eng, sel, it=it):
                eng.wait_ge(S["dve"].h, S["dve"].v)
                eng.wait_ge(S["act"].h, S["act"].v)
                if it > 0:
                    eng.wait_ge(S["gsn"].h, S["gsn"].v)
                    eng.wait_ge(S["z2m"].h, S["z2m"].v)
                k = 0
                for s in range(2):
                    for hh in range(2):
                        g = 2 * s + hh
                        for n in range(NSTEP):
                            for dst, srcv in (
                                (xks[32 * g:32 * g + 16, 128 * n:128 * (n + 1)],
                                 xkbt[s][16 * n:16 * n + 16, 128 * hh:128 * (hh + 1)]),
                                (gbts[32 * g:32 * g + 16, 128 * n:128 * (n + 1)],
                                 gbtt[s][16 * n:16 * n + 16, 128 * hh:128 * (hh + 1)]),
                                (sgs[32 * g:32 * g + 16, n:n + 1],
                                 sgm[s][16 * n:16 * n + 16, hh:hh + 1]),
                            ):
                                if k % 3 == sel:
                                    S["resh"].inc(eng.dma_start(dst, srcv), 16)
                                k += 1

            def ph_resh0(sync, it=it):
                mk_resh(sync, 0, it)
            block.sync(ph_resh0)

            def ph_resh1(gpsimd, it=it):
                mk_resh(gpsimd, 1, it)
            block.gpsimd(ph_resh1)

            def ph_resh2(scalar, it=it):
                mk_resh(scalar, 2, it)
            block.scalar(ph_resh2)

            def ph_resh_fin(sync):
                sync.wait_ge(S["resh"].h, S["resh"].v)
                S["resh"].inc(sync.nop())
            block.sync(ph_resh_fin)

            def ph_negsig(gpsimd):
                gpsimd.wait_ge(S["resh"].h, S["resh"].v)
                _li = gpsimd.tensor_single_scalar(sgs[:, 8:16], sgs[:, 0:8], -1.0, ALU.mult)
                S["gp"].inc(_li)
            block.gpsimd(ph_negsig)

            def ph_sp0(tensor, it=it):
                tensor.wait_ge(S["evB"].h, S["evB"].v)
                if it > 0:
                    tensor.wait_ge(S["msk"].h, S["msk"].v)
                spi = []
                for g in range(4):
                    s, hh = divmod(g, 2)
                    spi.append(MM(
                        tensor, psp[32 * g:32 * g + 16, 16 * g:16 * g + 16],
                        kqt[s][:, 256 * hh:256 * hh + 16],
                        kqt[s][:, 256 * hh + 16:256 * hh + 32],
                        start=True, stop=True,
                        tile_position=tpos(0, 32 * g)))
                S["sp"].inc(spi[-1])
            block.tensor(ph_sp0)

            def emit_expgroup(gi, buf=buf):
                pass_gi = gi
                s_, rem = divmod(gi, 16)
                hh_, e_ = divmod(rem, 8)
                bank = pe0 if (e_ % 2 == 0) else pe1

                def ph_exp_mm(tensor, s=s_, e=e_, hh=hh_, bank=bank, gi=gi):
                    if S["comb"].v >= 2:
                        tensor.wait_ge(S["comb"].h, S["comb"].v - 1)
                    if gi == 0:
                        tensor.wait_ge(S["dve"].h, S["dve"].v)
                        tensor.wait_ge(S["act"].h, S["act"].v)
                    if e == 0 and hh == 0:
                        tensor.wait_ge(S["w8s"].h, S["w8s"].v)
                    for c in range(8):
                        _li = MM(tensor, bank[:, 0:512],
                                 xnf[s][:, 128 * c:128 * (c + 1)],
                                 ewb[:, (c * 8 + e) * 1024 + 512 * hh:
                                     (c * 8 + e) * 1024 + 512 * (hh + 1)],
                                 start=(c == 0), stop=(c == 7))
                    S["expmm"].inc(_li)
                block.tensor(ph_exp_mm)

                def ph_comb(vector, s=s_, e=e_, hh=hh_, bank=bank, buf=buf):
                    vector.wait_ge(S["expmm"].h, S["expmm"].v)
                    if e == 0:
                        vector.wait_ge(S["w8s"].h, S["w8s"].v)
                    dst = xb[s][buf][:, 512 * hh:512 * (hh + 1)]
                    _li = vector.scalar_tensor_tensor(dst, bank[:, 0:512],
                                                      w8[s][:, e:e + 1], dst,
                                                      ALU.mult, ALU.add)
                    S["comb"].inc(_li)
                block.vector(ph_comb)

            for n in range(NSTEP):
                N = NSTEP * it + n

                def sc_mask(vector, N=N):
                    if "mask" in NEUTER:
                        vector.wait_ge(S["sp"].h, S["sp"].v)
                        _d = vector.memset(s4t[:, 0:128], 0.0)
                        S["msk"].inc(_d)
                        return
                    vector.wait_ge(S["sp"].h, S["sp"].v)
                    _li = vector.scalar_tensor_tensor(
                        s4t[:].rearrange("p (b c) -> p b c", c=32)[:, :, 16:32],
                        psp[:, 0:64].rearrange("p (b i) -> p b i", b=4),
                        1.0,
                        masktok[:].rearrange("p (b i) -> p b i", b=4),
                        ALU.add, ALU.mult)
                    S["msk"].inc(_li)
                if "mask" in SCEMIT:
                    block.vector(sc_mask)

                def sc_mma(tensor, N=N, n=n, it=it):
                    tensor.wait_ge(S["mir"].h, S["mir"].v)
                    if n == 0:
                        tensor.wait_ge(S["gp"].h, S["gp"].v)
                        tensor.wait_ge(S["resh"].h, S["resh"].v)
                        tensor.wait_ge(S["xus"].h, S["xus"].v)
                    tensor.wait_ge(S["xhb"].h, S["xhb"].v)
                    tensor.wait_ge(S["st2"].h, S["st2"].v)
                    insts = []
                    MM(tensor, pzc[:, 0:129], indb[:], b1eb[:, 0:129],
                       start=True, stop=False)
                    for g in range(4):
                        s, hh = divmod(g, 2)
                        lhs = kqt[s][:, 256 * hh + 32 * n:256 * hh + 32 * n + 32]
                        MM(tensor, pzc[32 * g:32 * (g + 1), 0:128], lhs,
                           w1s[:, 128 * g:128 * (g + 1)],
                           start=False, stop=False,
                           tile_position=tpos(0, 32 * g))
                        insts.append(MM(
                            tensor, pzc[32 * g:32 * (g + 1), 128:129], lhs,
                            rsw1b[:, g:g + 1],
                            start=False, stop=(g == 3),
                            tile_position=tpos(0, 32 * g)))
                    S["z1"].inc(insts[-1])
                if "mma" in SCEMIT:
                    block.tensor(sc_mma)

                def sc_st1a(scalar, N=N):
                    if "st1a" in NEUTER:
                        scalar.wait_ge(S["z1"].h, S["z1"].v)
                        _d = scalar.activation(scols[:, 0:1], scols[:, 0:1], AF.Copy)
                        S["st1a"].inc(_d)
                        return
                    scalar.wait_ge(S["z1"].h, S["z1"].v)
                    _li = scalar.activation(sw[0][:], pzc[:, 0:128], AF.Square,
                                      accum_out=scols[:, 0:1])
                    S["st1a"].inc(_li)
                if "st1a" in SCEMIT:
                    block.scalar(sc_st1a)

                def sc_st1x(vector, N=N):
                    vector.wait_ge(S["z1"].h, S["z1"].v)
                    _li = vector.tensor_copy(scols[:, 15:16], pzc[:, 128:129])
                    S["sx1"].inc(_li)
                block.vector(sc_st1x)

                def sc_st1(gpsimd, N=N):
                    if "st1" in NEUTER:
                        gpsimd.wait_ge(S["sx1"].h, S["sx1"].v)
                        gpsimd.wait_ge(S["st1a"].h, S["st1a"].v)
                        _d = gpsimd.memset(scols[:, 0:8], 1.0)
                        S["var1"].inc(_d)
                        return
                    gpsimd.wait_ge(S["sx1"].h, S["sx1"].v)
                    gpsimd.wait_ge(S["st1a"].h, S["st1a"].v)
                    gpsimd.tensor_single_scalar(scols[:, 1:2], scols[:, 15:16],
                                                INV128, ALU.mult)
                    gpsimd.tensor_single_scalar(scols[:, 2:3], scols[:, 0:1],
                                                INV128, ALU.mult)
                    gpsimd.tensor_mul(scols[:, 3:4], scols[:, 1:2], scols[:, 1:2])
                    gpsimd.tensor_sub(scols[:, 4:5], scols[:, 2:3], scols[:, 3:4])
                    _li = gpsimd.tensor_single_scalar(scols[:, 4:5], scols[:, 4:5],
                                                EPS6, ALU.add)
                    S["var1"].inc(_li)
                if "st1" in SCEMIT:
                    block.gpsimd(sc_st1)

                def sc_rstd1(scalar, N=N):
                    if "rstd1" in NEUTER:
                        scalar.wait_ge(S["var1"].h, S["var1"].v)
                        _d = scalar.activation(scols[:, 6:7], scols[:, 4:5], AF.Copy)
                        S["std1"].inc(_d)
                        return
                    scalar.wait_ge(S["var1"].h, S["var1"].v)
                    scalar.activation(scols[:, 5:6], scols[:, 4:5], AF.Ln)
                    _li = scalar.activation(scols[:, 6:7], scols[:, 5:6], AF.Exp,
                                      scale=-0.5)
                    S["std1"].inc(_li)
                if "rstd1" in SCEMIT:
                    block.scalar(sc_rstd1)

                def sc_chainA(vector, N=N, n=n):
                    if "chain" in NEUTER:
                        vector.wait_ge(S["std1"].h, S["std1"].v)
                        vector.memset(gsn[:], 0.0)
                        _d = vector.memset(tsg[:], 0.0)
                        S["gsn"].inc(_d)
                        return
                    vector.wait_ge(S["std1"].h, S["std1"].v)
                    vector.tensor_tensor(sw[1][:], pzc[:, 0:128],
                                         bc(scols[:, 1:2], 128), ALU.subtract)
                    vector.tensor_tensor(sw[2][:], sw[1][:],
                                         bc(scols[:, 6:7], 128), ALU.mult)
                    vector.tensor_mul(sw[3][:], sw[2][:], g2scan[:])
                    vector.scalar_tensor_tensor(sw[4][:], sw[3][:], 0.0,
                                                gbts[:, 128 * n:128 * (n + 1)],
                                                ALU.add, ALU.add,
                                                accum_out=c12[:, 0:1])
                    _li = vector.scalar_tensor_tensor(sw[0][:], sw[4][:], 1.0,
                                                sw[2][:], ALU.mult, ALU.mult,
                                                accum_out=c12[:, 1:2])
                    S["c12s"].inc(_li)
                    vector.tensor_tensor(sw[1][:], sw[4][:],
                                         bc(scols[:, 6:7], 128), ALU.mult)
                    vector.wait_ge(S["qrr"].h, S["qrr"].v + 1)
                    vector.tensor_tensor(sw[3][:], sw[2][:],
                                         bc(scols[:, 8:9], 128), ALU.mult)
                    vector.tensor_tensor(sw[4][:], sw[3][:],
                                         bc(scols[:, 7:8], 128), ALU.add)
                    vector.tensor_sub(sw[3][:], sw[4][:], sw[1][:])
                    vector.tensor_tensor(gsn[:], sw[3][:],
                                         bc(sgs[:, 8 + n:9 + n], 128), ALU.mult)
                    _li = vector.tensor_tensor(tsg[:], gsn[:],
                                         bc(ccol[:, 0:1], 128), ALU.mult)
                    S["gsn"].inc(_li)
                if "chain" in SCEMIT:
                    block.vector(sc_chainA)

                def sc_qrr(gpsimd, N=N):
                    gpsimd.wait_ge(S["c12s"].h, S["c12s"].v)
                    gpsimd.wait_ge(S["std1"].h, S["std1"].v)
                    gpsimd.tensor_single_scalar(scols[:, 7:8], c12[:, 0:1],
                                                INV128, ALU.mult)
                    gpsimd.tensor_mul(scols[:, 7:8], scols[:, 7:8], scols[:, 6:7])
                    gpsimd.tensor_single_scalar(scols[:, 8:9], c12[:, 1:2],
                                                INV128, ALU.mult)
                    _li = gpsimd.tensor_mul(scols[:, 8:9], scols[:, 8:9],
                                            scols[:, 6:7])
                    S["qrr"].inc(_li)
                if "chain" in SCEMIT:
                    block.gpsimd(sc_qrr)

                def sc_mm2(tensor, N=N, n=n):
                    if "mm2" in NEUTER:
                        tensor.wait_ge(S["gsn"].h, S["gsn"].v)
                        tensor.wait_ge(S["msk"].h, S["msk"].v)
                        S["z2m"].inc(nc.tensor.nop())
                        return
                    tensor.wait_ge(S["gsn"].h, S["gsn"].v)
                    tensor.wait_ge(S["msk"].h, S["msk"].v)
                    def w1mm(g, stop):
                        return MM(tensor, pw1[:, 128 * g:128 * (g + 1)],
                                  xks[32 * g:32 * g + 16, 128 * n:128 * (n + 1)],
                                  tsg[32 * g:32 * g + 16, :],
                                  start=False, stop=stop,
                                  tile_position=tpos(32 * g, 0))
                    MM(tensor, pzc[:, 0:128], s4t[:], gsn[:],
                       start=False, stop=False)
                    w1mm(0, False)
                    MM(tensor, pzc[0:4, 384:512], ind2b[:], tsg[:],
                       start=True, stop=True)
                    w1mm(1, False)
                    MM(tensor, pzc[0:4, 448:512], zb4[:], tsg[:, 0:64],
                       start=False, stop=False)
                    w1mm(2, False)
                    MM(tensor, pzc[0:4, 448:512], zb4[:], tsg[:, 0:64],
                       start=False, stop=True)
                    _li = w1mm(3, True)
                    S["z2m"].inc(_li)
                if "mm2" in SCEMIT:
                    block.tensor(sc_mm2)

                if n + 1 < NSTEP:
                    def sc_spn(tensor, N=N, n=n):
                        if "spn" in NEUTER:
                            tensor.wait_ge(S["msk"].h, S["msk"].v)
                            S["sp"].inc(nc.tensor.nop())
                            return
                        tensor.wait_ge(S["msk"].h, S["msk"].v)
                        spi = []
                        for g in range(4):
                            s, hh = divmod(g, 2)
                            spi.append(MM(
                                tensor,
                                psp[32 * g:32 * g + 16, 16 * g:16 * g + 16],
                                kqt[s][:, 256 * hh + 32 * (n + 1):
                                       256 * hh + 32 * (n + 1) + 16],
                                kqt[s][:, 256 * hh + 32 * (n + 1) + 16:
                                       256 * hh + 32 * (n + 1) + 32],
                                start=True, stop=True,
                                tile_position=tpos(0, 32 * g)))
                        S["sp"].inc(spi[-1])
                    if "spn" in SCEMIT:
                        block.tensor(sc_spn)

                def sc_act2(scalar, N=N):
                    if "act2" in NEUTER:
                        scalar.wait_ge(S["z2m"].h, S["z2m"].v)
                        _d = scalar.activation(scols[:, 10:11], scols[:, 10:11], AF.Copy)
                        S["st2"].inc(_d)
                        _d = scalar.activation(rsw1b[:, 0:1], rsw1[:, 0:1], AF.Copy)
                        S["mir"].inc(_d)
                        return
                    scalar.wait_ge(S["z2m"].h, S["z2m"].v)
                    _li = scalar.activation(sw[0][:], pzc[:, 0:128], AF.Square,
                                      accum_out=scols[:, 10:11])
                    S["st2"].inc(_li)
                    for g in range(4):
                        scalar.activation(w1s[:, 128 * g:128 * (g + 1)],
                                          pw1[:, 128 * g:128 * (g + 1)], AF.Copy,
                                          accum_out=rsw1[:, g:g + 1])
                    _li = scalar.activation(rsw1b[:], rsw1[:], AF.Copy)
                    S["mir"].inc(_li)
                if "act2" in SCEMIT:
                    block.scalar(sc_act2)

                def sc_st2d(gpsimd, N=N):
                    gpsimd.wait_ge(S["st2"].h, S["st2"].v)
                    gpsimd.tensor_single_scalar(scols[:, 11:12], scols[:, 10:11],
                                                INV128, ALU.mult)
                    gpsimd.tensor_sub(scols[:, 11:12], scols[:, 11:12],
                                      scols[:, 3:4])
                    _li = gpsimd.tensor_single_scalar(scols[:, 11:12],
                                                scols[:, 11:12], EPS6, ALU.add)
                    S["var2"].inc(_li)
                if "st2d" in SCEMIT:
                    block.gpsimd(sc_st2d)

                def sc_rstd2(scalar, N=N):
                    if "rstd2" in NEUTER:
                        scalar.wait_ge(S["var2"].h, S["var2"].v)
                        _d = scalar.activation(scols[:, 13:14], scols[:, 11:12], AF.Copy)
                        S["std2"].inc(_d)
                        return
                    scalar.wait_ge(S["var2"].h, S["var2"].v)
                    scalar.activation(scols[:, 12:13], scols[:, 11:12], AF.Ln)
                    _li = scalar.activation(scols[:, 13:14], scols[:, 12:13], AF.Exp,
                                      scale=-0.5)
                    S["std2"].inc(_li)
                if "rstd2" in SCEMIT:
                    block.scalar(sc_rstd2)

                def sc_xhb(vector, N=N, n=n):
                    if "xhb" in NEUTER:
                        vector.wait_ge(S["std2"].h, S["std2"].v)
                        _d = vector.memset(xhbs[:, 128 * n:128 * (n + 1)], 0.0)
                        S["xhb"].inc(_d)
                        return
                    vector.wait_ge(S["std2"].h, S["std2"].v)
                    vector.tensor_tensor(sw[1][:], pzc[:, 0:128],
                                         bc(scols[:, 1:2], 128), ALU.subtract)
                    vector.tensor_tensor(xhbs[:, 128 * n:128 * (n + 1)], sw[1][:],
                                         bc(scols[:, 13:14], 128), ALU.mult)
                    vector.tensor_add(b1c[:], b1c[:], pzc[0:4, 384:512])
                    vector.reduce_sum(b1rs[:], b1c[:], axis=AX)
                    vector.tensor_copy(b1eb[:, 0:128], b1c[:])
                    _li = vector.tensor_copy(b1eb[:, 128:129], b1rs[:])
                    S["xhb"].inc(_li)
                if "xhb" in SCEMIT:
                    block.vector(sc_xhb)

                for gi in range(3 * n, 3 * n + 3):
                    emit_expgroup(gi)

            def mk_xout(eng, sel, it=it):
                eng.wait_ge(S["xhb"].h, S["xhb"].v)
                eng.wait_ge(S["dve"].h, S["dve"].v)
                k = 0
                for s in range(2):
                    for hh in range(2):
                        g = 2 * s + hh
                        for n in range(NSTEP):
                            if k % 2 == sel:
                                S["xus"].inc(eng.dma_start(
                                    xhbt[s][16 * n:16 * n + 16, 128 * hh:128 * (hh + 1)],
                                    xhbs[32 * g + 16:32 * g + 32, 128 * n:128 * (n + 1)]), 16)
                            k += 1

            def ph_xout0(sync, it=it):
                mk_xout(sync, 0, it)
            block.sync(ph_xout0)

            def ph_xout1(gpsimd, it=it):
                mk_xout(gpsimd, 1, it)
            block.gpsimd(ph_xout1)

            def ph_xout_fin(sync):
                sync.wait_ge(S["xus"].h, S["xus"].v)
                S["xus"].inc(sync.nop())
            block.sync(ph_xout_fin)

            for s in range(2):
                def ph_outs_dve(vector, s=s):
                    vector.wait_ge(S["xus"].h, S["xus"].v)
                    vector.tensor_mul(ows[s][:], xhbt[s][:], gw_rep[:])
                    vector.tensor_add(ows[s][:], ows[s][:], b_rep[:])
                    _li = vector.tensor_add(ows[s][:], ows[s][:], xqt[s][:])
                    S["dve"].inc(_li)
                block.vector(ph_outs_dve)

                def ph_lnp_act(scalar, s=s):
                    scalar.wait_ge(S["dve"].h, S["dve"].v)
                    b0 = 16 + 8 * s
                    scalar.activation(gbtt[s][:], ows[s][:], AF.Copy,
                                      accum_out=lnc[:, b0:b0 + 1])
                    _li = scalar.activation(gbtt[s][:], ows[s][:], AF.Square,
                                      accum_out=lnc[:, b0 + 1:b0 + 2])
                    S["act"].inc(_li)
                block.scalar(ph_lnp_act)

                def ph_lnp_gp2(gpsimd, s=s):
                    gpsimd.wait_ge(S["act"].h, S["act"].v)
                    b0 = 16 + 8 * s
                    gpsimd.tensor_single_scalar(lnc[:, b0 + 2:b0 + 3],
                                                lnc[:, b0:b0 + 1], 1.0 / DD, ALU.mult)
                    gpsimd.tensor_single_scalar(lnc[:, b0 + 3:b0 + 4],
                                                lnc[:, b0 + 1:b0 + 2], 1.0 / DD,
                                                ALU.mult)
                    gpsimd.tensor_mul(lnc[:, b0 + 4:b0 + 5], lnc[:, b0 + 2:b0 + 3],
                                      lnc[:, b0 + 2:b0 + 3])
                    _li = gpsimd.tensor_sub(lnc[:, b0 + 5:b0 + 6], lnc[:, b0 + 3:b0 + 4],
                                      lnc[:, b0 + 4:b0 + 5])
                    S["gp"].inc(_li)
                block.gpsimd(ph_lnp_gp2)

                def ph_lnp_act2(scalar, s=s):
                    scalar.wait_ge(S["gp"].h, S["gp"].v)
                    b0 = 16 + 8 * s
                    scalar.activation(lnc[:, b0 + 5:b0 + 6], lnc[:, b0 + 5:b0 + 6],
                                      AF.Ln, bias=ccol[:, 2:3])
                    _li = scalar.activation(lnc[:, b0 + 6:b0 + 7], lnc[:, b0 + 5:b0 + 6],
                                      AF.Exp, scale=-0.5)
                    S["act"].inc(_li)
                block.scalar(ph_lnp_act2)

                def ph_lnp_dve(vector, s=s):
                    vector.wait_ge(S["act"].h, S["act"].v)
                    b0 = 16 + 8 * s
                    _li = vector.tensor_scalar(kqt[s][:, 256:512], ows[s][:],
                                         lnc[:, b0 + 2:b0 + 3], lnc[:, b0 + 6:b0 + 7],
                                         ALU.subtract, ALU.mult)
                    S["dve"].inc(_li)
                block.vector(ph_lnp_dve)

                def ph_wo_te(tensor, s=s):
                    tensor.wait_ge(S["dve"].h, S["dve"].v)
                    for c in range(2):
                        _li = tensor.transpose(put_bf[:, 512 + 128 * c:512 + 128 * (c + 1)],
                                         kqt[s][:, 256 + 128 * c:256 + 128 * (c + 1)], idb[:])
                    S["te"].inc(_li)
                block.tensor(ph_wo_te)

                def ph_wo_ev(vector, s=s):
                    vector.wait_ge(S["te"].h, S["te"].v)
                    for c in range(2):
                        _li = vector.tensor_copy(kqt[s][:, 128 * c:128 * (c + 1)],
                                           put_bf[:, 512 + 128 * c:512 + 128 * (c + 1)])
                    S["evB"].inc(_li)
                block.vector(ph_wo_ev)

                def ph_wo_mm(tensor, s=s):
                    tensor.wait_ge(S["evB"].h, S["evB"].v)
                    tensor.wait_ge(S["act"].h, S["act"].v)
                    tensor.wait_ge(S["dve"].h, S["dve"].v)
                    for c in range(2):
                        MM(tensor, put[:, 0:256], kqt[s][:, 128 * c:128 * (c + 1)],
                                      wob[:, 256 * c:256 * (c + 1)],
                                      start=(c == 0), stop=False)
                    _li = MM(tensor, put[:, 0:256], ones1[:], rowbpw[0:1, 0:256],
                                  start=False, stop=True)
                    S["te"].inc(_li)
                block.tensor(ph_wo_mm)

                for gi in range(24 + 4 * s, 24 + 4 * s + 2):
                    emit_expgroup(gi)

                def ph_ln2_act(scalar, s=s):
                    scalar.wait_ge(S["te"].h, S["te"].v)
                    b0 = 32 + 8 * s
                    scalar.activation(ows[s][:], put[:, 0:256], AF.Copy,
                                      accum_out=lnc[:, b0:b0 + 1])
                    _li = scalar.activation(ows[s][:], put[:, 0:256], AF.Square,
                                      accum_out=lnc[:, b0 + 1:b0 + 2])
                    S["act"].inc(_li)
                block.scalar(ph_ln2_act)

                def ph_ln2_gp(gpsimd, s=s):
                    gpsimd.wait_ge(S["act"].h, S["act"].v)
                    b0 = 32 + 8 * s
                    gpsimd.tensor_single_scalar(lnc[:, b0 + 2:b0 + 3],
                                                lnc[:, b0:b0 + 1], 1.0 / DD, ALU.mult)
                    gpsimd.tensor_single_scalar(lnc[:, b0 + 3:b0 + 4],
                                                lnc[:, b0 + 1:b0 + 2], 1.0 / DD,
                                                ALU.mult)
                    gpsimd.tensor_mul(lnc[:, b0 + 4:b0 + 5], lnc[:, b0 + 2:b0 + 3],
                                      lnc[:, b0 + 2:b0 + 3])
                    _li = gpsimd.tensor_sub(lnc[:, b0 + 5:b0 + 6], lnc[:, b0 + 3:b0 + 4],
                                      lnc[:, b0 + 4:b0 + 5])
                    S["gp"].inc(_li)
                block.gpsimd(ph_ln2_gp)

                def ph_ln2_act2(scalar, s=s):
                    scalar.wait_ge(S["gp"].h, S["gp"].v)
                    b0 = 32 + 8 * s
                    scalar.activation(lnc[:, b0 + 5:b0 + 6], lnc[:, b0 + 5:b0 + 6],
                                      AF.Ln, bias=ccol[:, 2:3])
                    _li = scalar.activation(lnc[:, b0 + 6:b0 + 7], lnc[:, b0 + 5:b0 + 6],
                                      AF.Exp, scale=-0.5)
                    S["act"].inc(_li)
                block.scalar(ph_ln2_act2)

                def ph_ln2_dve(vector, s=s):
                    vector.wait_ge(S["act"].h, S["act"].v)
                    b0 = 32 + 8 * s
                    _li = vector.tensor_scalar(kqt[s][:, 256:512], put[:, 0:256],
                                         lnc[:, b0 + 2:b0 + 3], lnc[:, b0 + 6:b0 + 7],
                                         ALU.subtract, ALU.mult)
                    S["dve"].inc(_li)
                block.vector(ph_ln2_dve)

                def ph_up_te(tensor, s=s):
                    tensor.wait_ge(S["dve"].h, S["dve"].v)
                    tensor.wait_ge(S["act"].h, S["act"].v)
                    tensor.wait_ge(S["w8s"].h, S["w8s"].v)
                    for c in range(2):
                        tensor.transpose(put_bf[:, 512 + 128 * c:512 + 128 * (c + 1)],
                                         kqt[s][:, 256 + 128 * c:256 + 128 * (c + 1)], idb[:])
                    _li = tensor.transpose(put_bf[0:8, 768:896], w8bf[s][:], idb[:])
                    S["te"].inc(_li)
                block.tensor(ph_up_te)

                def ph_up_ev(vector, s=s):
                    vector.wait_ge(S["te"].h, S["te"].v)
                    for c in range(2):
                        vector.tensor_copy(kqt[s][:, 128 * c:128 * (c + 1)],
                                           put_bf[:, 512 + 128 * c:512 + 128 * (c + 1)])
                    _li = vector.tensor_copy(w8tb[:], put_bf[0:8, 768:896])
                    S["evB"].inc(_li)
                block.vector(ph_up_ev)

                def ph_up_mm(tensor, s=s):
                    tensor.wait_ge(S["evB"].h, S["evB"].v)
                    tensor.wait_ge(S["comb"].h, S["comb"].v)
                    for hh, pu in enumerate([pu0, pu1]):
                        for c in range(2):
                            MM(tensor, 
                                pu[:, 0:512], kqt[s][:, 128 * c:128 * (c + 1)],
                                upb[:, 1024 * c + 512 * hh:1024 * c + 512 * (hh + 1)],
                                start=(c == 0), stop=False)
                        MM(tensor, pu[:, 0:512], ones1[:],
                                      rowup[0:1, 512 * hh:512 * (hh + 1)],
                                      start=False, stop=False)
                        _li = MM(tensor, pu[:, 0:512], w8tb[:],
                                      expb[:, 512 * hh:512 * (hh + 1)],
                                      start=False, stop=True)
                    S["phU"].inc(_li)
                block.tensor(ph_up_mm)

                for gi in range(24 + 4 * s + 2, 24 + 4 * s + 4):
                    emit_expgroup(gi)

                for hh in range(2):
                    def ph_yfin(vector, s=s, hh=hh, buf=buf):
                        vector.wait_ge(S["comb"].h, S["comb"].v)
                        vector.wait_ge(S["phU"].h, S["phU"].v)
                        pu = pu0 if hh == 0 else pu1
                        _li = vector.tensor_add(xb[s][buf][:, 512 * hh:512 * (hh + 1)],
                                          xb[s][buf][:, 512 * hh:512 * (hh + 1)],
                                          pu[:, 0:512])
                        S["comb"].inc(_li)
                    block.vector(ph_yfin)

                def ph_sto(gpsimd, s=s, it=it):
                    gpsimd.wait_ge(S["comb"].h, S["comb"].v)
                    S["sto"].inc(gpsimd.dma_start(
                        y[s, 128 * it:128 * (it + 1), :], xb[s][it % 2][:]), 16)
                block.gpsimd(ph_sto)

            if it + 2 < NT:
                def sec_ldx_late(sync, it=it):
                    sync.wait_ge(S["sto"].h, S["sto"].v)
                    for s in range(2):
                        S["ldx"].inc(sync.dma_start(
                            xb[s][it % 2][:],
                            d["x"][s, 128 * (it + 2):128 * (it + 3), :]), 16)
                block.sync(sec_ldx_late)

        if DBG:
            dbgs = {}
            for nm, t in [("xnb0", xnb[0]), ("htb0", htb[0]), ("xqt0", xqt[0]),
                          ("xkt0", xkt[0]), ("xvt0", xvt[0]), ("sgm0", sgm[0]),
                          ("glog0", glog[0]), ("w80", w8[0]), ("xks", xks),
                          ("gbts", gbts), ("sgs", sgs), ("xhbs", xhbs),
                          ("xhbt0", xhbt[0]), ("w1s", w1s), ("b1c", b1c),
                          ("kqt0", kqt[0]), ("xnf0", xnf[0]), ("s4t", s4t),
                          ("masktok", masktok), ("g2scan", g2scan),
                          ("tsg", tsg), ("gsn", gsn), ("ows0", ows[0]), ("w81", w8[1]), ("glog1", glog[1]),
                          ("gbtt0", gbtt[0])]:
                shp = list(t.shape) if hasattr(t, "shape") else None
                dt_ = {2: BF16}.get(0)
                dbgs[nm] = (nc.dram_tensor("dbg_" + nm, list(t[:].shape),
                                           t[:].dtype, kind="ExternalOutput").ap(), t)

            def sec_dbg(sync):
                sync.wait_ge(S["sto"].h, S["sto"].v)
                for nm, (dr, t) in dbgs.items():
                    S["dbg"].inc(sync.dma_start(dr, t[:]), 16)
                sync.wait_ge(S["dbg"].h, S["dbg"].v)
            block.sync(sec_dbg)

        def sec_fin(sync):
            sync.wait_ge(S["sto"].h, S["sto"].v)
        block.sync(sec_fin)

    return nc


_NC_CACHE = None


def kernel(**inp):
    global _NC_CACHE

    def f32(a):
        return np.ascontiguousarray(np.asarray(a), dtype=np.float32)

    x = f32(inp["L0_attn_x"])
    down_w = f32(inp["L0_attn_down_w"]); down_b = f32(inp["L0_attn_down_b"]).reshape(1, DD)
    up_w = f32(inp["L0_attn_up_w"]); up_b = f32(inp["L0_attn_up_b"]).reshape(1, DM)
    wq = f32(inp["L0_attn_wq"]); wk = f32(inp["L0_attn_wk"])
    wv = f32(inp["L0_attn_wv"]); wo = f32(inp["L0_attn_wo"])
    W1 = f32(inp["L0_attn_W1"])
    b1 = f32(inp["L0_attn_b1"]).reshape(2, 128)
    ttt_ln_w = f32(inp["L0_attn_ttt_ln_w"]).reshape(2, 128)
    ttt_ln_b = f32(inp["L0_attn_ttt_ln_b"]).reshape(2, 128)
    lr_w = f32(inp["L0_attn_lr_w"]).reshape(2, DD)
    lr_b = f32(inp["L0_attn_lr_b"]).reshape(1, 2)
    tok_idx = f32(inp["L0_attn_tok_idx"]).reshape(1, 16)
    post_w = f32(inp["L0_attn_post_w"]).reshape(2, 128)
    post_b = f32(inp["L0_attn_post_b"]).reshape(2, 128)
    ln_w = f32(inp["L0_attn_ln_w"]).reshape(2, 128)
    ln_b = f32(inp["L0_attn_ln_b"]).reshape(2, 128)
    gate_w = f32(inp["L0_ffn_gate_w"]); gate_b = f32(inp["L0_ffn_gate_b"]).reshape(1, 8)
    expert_w = f32(inp["L0_ffn_expert_w"]); expert_b = f32(inp["L0_ffn_expert_b"])

    w1init = np.ascontiguousarray(W1.transpose(1, 0, 2).reshape(128, 256))
    b1init = np.ascontiguousarray(b1[[0, 1, 0, 1], :])
    lr_wT = np.ascontiguousarray(lr_w.T)
    gate_wT = np.ascontiguousarray(gate_w.T)
    ew_t = np.ascontiguousarray(expert_w.transpose(0, 2, 1))
    tokrep = np.ascontiguousarray(np.broadcast_to(tok_idx, (128, 16)))
    ttt_row_w = ttt_ln_w.reshape(1, 256)
    ttt_row_b = ttt_ln_b.reshape(1, 256)
    gw_rep = np.ascontiguousarray(np.broadcast_to(ttt_row_w, (128, 256)))
    b_rep = np.ascontiguousarray(np.broadcast_to(ttt_row_b, (128, 256)))
    gscan0 = np.zeros((128, 128), dtype=np.float32)
    for g in range(4):
        gscan0[32 * g:32 * g + 16, :] = ttt_ln_w[g % 2][None, :]

    import ml_dtypes
    BF = ml_dtypes.bfloat16
    identity_bf = np.eye(128, dtype=BF)
    ones_row_bf = np.ones((1, 128), dtype=BF)
    ind_bf = np.zeros((4, 128), dtype=BF)
    ind2_bf = np.zeros((128, 4), dtype=BF)
    smask = np.zeros((128, 64), dtype=np.float32)
    for g in range(4):
        ind_bf[g, 32 * g:32 * (g + 1)] = 1
        for j in range(16):
            ind2_bf[32 * g + j, g] = 1
            smask[32 * g + j, 16 * g + j:16 * (g + 1)] = 1
    invkrep = np.ascontiguousarray(np.broadcast_to(
        (1.0 / np.arange(1, 17, dtype=np.float32)).reshape(1, 16), (128, 16)))

    shared = dict(
        down_w=down_w, down_b=down_b, up_w=up_w, up_b=up_b,
        wq=wq, wk=wk, wv=wv, wo=wo, w1init=w1init, b1init=b1init,
        lr_wT=lr_wT, lr_b=lr_b, tokrep=tokrep, invkrep=invkrep,
        post_w_c=np.ascontiguousarray(post_w.T), post_b_c=np.ascontiguousarray(post_b.T),
        ln_w_c=np.ascontiguousarray(ln_w.T), ln_b_c=np.ascontiguousarray(ln_b.T),
        gate_wT=gate_wT, gate_b=gate_b, ew_t=ew_t, expert_b=expert_b,
        gw_rep=gw_rep, b_rep=b_rep, gscan0=gscan0,
        identity_bf=identity_bf, ones_row_bf=ones_row_bf,
        ind_bf=ind_bf, ind2_bf=ind2_bf, smask=smask,
    )

    if _NC_CACHE is None:
        _NC_CACHE = build_nc()
    nc = _NC_CACHE

    in_maps = []
    for c in range(8):
        m = dict(shared)
        m["x"] = np.ascontiguousarray(x[2 * c:2 * c + 2])
        in_maps.append(m)

    res = run_bass_kernel_spmd(nc, in_maps, core_ids=list(range(8)))
    out = np.empty((16, SEQ, DM), dtype=np.float32)
    for c in range(8):
        out[2 * c:2 * c + 2] = res.results[c]["y"]
    return out


# version 05
# speedup vs baseline: 1.0259x; 1.0259x over previous
import numpy as np
import concourse.bass as bass
import concourse.mybir as mybir
from concourse.bass_utils import run_bass_kernel_spmd
from contextlib import ExitStack

F32 = mybir.dt.float32
BF16 = mybir.dt.bfloat16
ALU = mybir.AluOpType
AF = mybir.ActivationFunctionType
AX = mybir.AxisListType.X

SEQ = 4096
DM = 1024
DD = 256
NT = SEQ // 128
NSTEP = 8
DBG = False
SCEMIT = frozenset(["mma", "st1a", "st1", "rstd1", "mask", "chain",
                    "mm2", "spn", "act2", "st2d", "rstd2", "xhb"])
NEUTER = frozenset()
INV128 = 1.0 / 128.0
EPS6 = 1e-6
LN_INV128 = -4.852030263919617
EPS5 = 1e-5
BIG = 1e30


class Sem:
    def __init__(self, ctx, nc, name):
        self.h = ctx.enter_context(nc.semaphore(name))
        self.v = 0

    def inc(self, inst, k=1):
        inst.then_inc(self.h, k)
        self.v += k


def bc(col, n):
    return col.broadcast_to([col.shape[0], n])


def tpos(row_base, col_base):
    if row_base == 0 and col_base == 0:
        return None
    return (row_base, col_base)


def MM(te, out, lhsT, rhs, **kw):
    kw.setdefault("skip_group_check", True)
    return te.matmul(out, lhsT, rhs, **kw)


def build_nc():
    nc = bass.Bass("TRN2", target_bir_lowering=False)
    d = {}

    def din(name, shape, dt=F32):
        d[name] = nc.dram_tensor(name, shape, dt, kind="ExternalInput").ap()

    din("x", [2, SEQ, DM])
    din("down_w", [DM, DD]); din("down_b", [1, DD])
    din("up_w", [DD, DM]); din("up_b", [1, DM])
    din("wq", [DD, DD]); din("wk", [DD, DD]); din("wv", [DD, DD]); din("wo", [DD, DD])
    din("w1init", [128, 256])
    din("b1init", [4, 128])
    din("lr_wT", [DD, 2]); din("lr_b", [1, 2])
    din("tokrep", [128, 16])
    din("invkrep", [128, 16])
    din("post_w_c", [128, 2]); din("post_b_c", [128, 2])
    din("ln_w_c", [128, 2]); din("ln_b_c", [128, 2])
    din("gate_wT", [DM, 8]); din("gate_b", [1, 8])
    din("ew_t", [8, DM, DM]); din("expert_b", [8, DM])
    din("gw_rep", [128, 256])
    din("b_rep", [128, 256])
    din("gscan0", [128, 128])
    din("identity_bf", [128, 128], BF16)
    din("ones_row_bf", [1, 128], BF16)
    din("ind_bf", [4, 128], BF16)
    din("ind2_bf", [128, 4], BF16)
    din("smask", [128, 64])
    y = nc.dram_tensor("y", [2, SEQ, DM], F32, kind="ExternalOutput").ap()

    ctx = ExitStack()
    with ctx:
        def sb(name, shape, dt=F32):
            return ctx.enter_context(nc.sbuf_tensor(name, shape, dt))

        def psf(name):
            return ctx.enter_context(nc.psum_tensor(name, [128, 512], F32))

        pw1 = psf("pw1")
        pzc = psf("pzc")
        psp = psf("psp")
        pe0 = psf("pe0")
        pe1 = psf("pe1")
        pu0 = psf("pu0")
        pu1 = psf("pu1")
        put = psf("put")
        put_bf = put[:].bitcast(BF16)
        pe1_bf = pe1[:].bitcast(BF16)

        ewb = sb("ewb", [128, 65536], BF16)
        dwb = sb("dwb", [128, 2048], BF16)
        upb = sb("upb", [128, 2048], BF16)
        wqb = sb("wqb", [128, 512], BF16)
        wkb = sb("wkb", [128, 512], BF16)
        wvb = sb("wvb", [128, 512], BF16)
        wob = sb("wob", [128, 512], BF16)
        gwb = sb("gwb", [128, 64], BF16)
        lrwb = sb("lrwb", [128, 4], BF16)
        expb = sb("expb", [8, 1024], BF16)
        rowsb = sb("rowsb", [1, 266], BF16)
        rowbpw = sb("rowbpw", [1, 256], BF16)
        rowup = sb("rowup", [1, 1024], BF16)
        w8tb = sb("w8tb", [8, 128], BF16)
        w8bf = [sb(f"w8bf{s}", [128, 8], BF16) for s in range(2)]
        idb = sb("idb", [128, 128], BF16)
        ones1 = sb("ones1", [1, 128], BF16)
        indb = sb("indb", [4, 128], BF16)
        ind2b = sb("ind2b", [128, 4], BF16)
        zb4 = sb("zb4", [128, 4], BF16)
        masktok = sb("masktok_s", [128, 64])
        g2scan = sb("g2scan_s", [128, 128])
        gw_rep = sb("gw_rep_s", [128, 256])
        b_rep = sb("b_rep_s", [128, 256])
        ccol = sb("ccol", [128, 4])
        pwcol = sb("pwcol", [128, 2])
        lwcol = sb("lwcol", [128, 2])
        pbcolb = sb("pbcolb", [128, 2], BF16)
        lbcolb = sb("lbcolb", [128, 2], BF16)

        w1s = sb("w1s", [128, 512], BF16)
        rsw1 = sb("rsw1", [128, 4])
        rsw1b = sb("rsw1b", [128, 4], BF16)
        b1c = sb("b1c", [4, 128])
        b1rs = sb("b1rs", [4, 1])
        b1eb = sb("b1eb", [4, 132], BF16)

        xb = [[sb(f"xb{s}{i}", [128, DM]) for i in range(2)] for s in range(2)]
        xnb = [sb(f"xnb{s}", [128, DM], BF16) for s in range(2)]
        xnf = [sb(f"xnf{s}", [128, DM], BF16) for s in range(2)]
        htb = [sb(f"htb{s}", [128, DD], BF16) for s in range(2)]
        hfm = [sb(f"hfm{s}", [128, DD], BF16) for s in range(2)]
        xqt = [sb(f"xqt{s}", [128, DD]) for s in range(2)]
        xkt = [sb(f"xkt{s}", [128, DD]) for s in range(2)]
        xvt = [sb(f"xvt{s}", [128, DD]) for s in range(2)]
        xqbt = [sb(f"xqbt{s}", [128, DD], BF16) for s in range(2)]
        xkbt = [sb(f"xkbt{s}", [128, DD], BF16) for s in range(2)]
        kqt = [sb(f"kqt{s}", [128, 512], BF16) for s in range(2)]
        sgm = [sb(f"sgm{s}", [128, 4]) for s in range(2)]
        gbtt = [sb(f"gbtt{s}", [128, DD]) for s in range(2)]
        xks = sb("xks", [128, NSTEP * 128], BF16)
        gbts = sb("gbts", [128, NSTEP * 128])
        sgs = sb("sgs", [128, 16])
        xhbs = sb("xhbs", [128, NSTEP * 128])
        s4t = sb("s4t", [128, 128], BF16)
        gsn = sb("gsn", [128, 128], BF16)
        tsg = sb("tsg", [128, 128], BF16)
        sw = [sb(f"sw{i}", [128, 128]) for i in range(5)]
        scols = sb("scols", [128, 16])
        c12 = sb("c12", [128, 2])
        xhbt = [sb(f"xhbt{s}", [128, DD]) for s in range(2)]
        ows = [sb(f"ows{s}", [128, DD]) for s in range(2)]
        glog = [sb(f"glog{s}", [128, 8]) for s in range(2)]
        gsc = [sb(f"gsc{s}", [128, 32]) for s in range(2)]
        w8 = [sb(f"w8{s}", [128, 8]) for s in range(2)]
        lnc = sb("lnc", [128, 48])

        S = {}
        for name in ["ldx", "ldw", "ldw2", "ldsm", "stp", "te", "dve", "act", "gp", "resh",
                     "sto", "z1", "gsn", "z2m", "mir", "var1", "std1",
                     "var2", "std2", "st2", "st1a", "msk", "sp", "xhb", "evA", "dbg",
                     "sx1", "c12s", "qrr", "ve", "gpb", "vqr", "b1u",
                     "evB", "phU", "comb", "expmm", "w8s", "xus"]:
            S[name] = Sem(ctx, nc, "s_" + name)

        block = ctx.enter_context(nc.Block())

        jobs = []
        for k in range(8):
            for e in range(8):
                jobs.append((d["ew_t"][e, 128 * k:128 * (k + 1), :], 1024,
                             ewb[:, (k * 8 + e) * 1024:(k * 8 + e + 1) * 1024]))
        for c in range(8):
            jobs.append((d["down_w"][128 * c:128 * (c + 1), :], 256,
                         dwb[:, 256 * c:256 * (c + 1)]))
        for c in range(2):
            jobs.append((d["up_w"][128 * c:128 * (c + 1), :], 1024,
                         upb[:, 1024 * c:1024 * (c + 1)]))
        for nm, dst in [("wq", wqb), ("wk", wkb), ("wv", wvb), ("wo", wob)]:
            for c in range(2):
                jobs.append((d[nm][128 * c:128 * (c + 1), :], 256,
                             dst[:, 256 * c:256 * (c + 1)]))
        for c in range(8):
            jobs.append((d["gate_wT"][128 * c:128 * (c + 1), :], 8,
                         gwb[:, 8 * c:8 * (c + 1)]))
        for c in range(2):
            jobs.append((d["lr_wT"][128 * c:128 * (c + 1), :], 2,
                         lrwb[:, 2 * c:2 * (c + 1)]))

        for i, (src, ncols, dst) in enumerate(jobs):
            stg = xb[0][i % 2]
            lsem = S["ldw"] if i % 2 == 0 else S["ldw2"]

            def sec_load(sync, src=src, ncols=ncols, stg=stg, i=i, lsem=lsem):
                if i >= 2:
                    sync.wait_ge(S["stp"].h, i - 1)
                lsem.inc(sync.dma_start(stg[:, 0:ncols], src), 16)
            block.sync(sec_load)

            def sec_cast(vector, ncols=ncols, dst=dst, stg=stg, lsem=lsem):
                vector.wait_ge(lsem.h, lsem.v)
                S["stp"].inc(vector.tensor_copy(dst, stg[:, 0:ncols]))
            block.vector(sec_cast)
        njobs = len(jobs)

        def sec_small(sync):
            for dst, src in [
                (idb[:], d["identity_bf"]), (ones1[:], d["ones_row_bf"]),
                (indb[:], d["ind_bf"]), (ind2b[:], d["ind2_bf"]),
                (masktok[:], d["smask"]),
                (pwcol[:], d["post_w_c"]), (lwcol[:], d["ln_w_c"]),
                (ccol[:, 0:2] if False else sw[2][:, 0:2], d["post_b_c"]),
                (sw[2][:, 2:4], d["ln_b_c"]),
                (b1c[:], d["b1init"]),
                (xhbs[32:33, 0:1024], d["up_b"]),
                (gw_rep[:], d["gw_rep"]), (b_rep[:], d["b_rep"]),
                (g2scan[:], d["gscan0"]),
                (sw[2][:, 4:20], d["tokrep"]), (sw[3][:, 0:16], d["invkrep"]),
                (gsc[0][0:1, 0:8], d["gate_b"]),
                (gsc[1][0:1, 0:2], d["lr_b"]),
                (gbtt[0][0:1, 0:256], d["down_b"]),
                (xhbs[0:8, 0:1024], d["expert_b"]),
                (gbtt[1][0:128, 0:128], d["w1init"][:, 0:128]),
                (xhbt[0][0:128, 0:128], d["w1init"][:, 128:256]),
            ]:
                S["ldsm"].inc(sync.dma_start(dst, src), 16)
        block.sync(sec_small)

        def sec_set1(vector):
            vector.wait_ge(S["ldsm"].h, S["ldsm"].v)
            vector.wait_ge(S["stp"].h, njobs)
            vector.memset(s4t[:], 0.0)
            vector.memset(zb4[:], 0.0)
            vector.memset(psp[:, 0:64], 0.0)
            vector.memset(gbts[:], 0.0)
            vector.memset(sgs[:], 0.0)
            vector.memset(xks[:], 0.0)
            vector.memset(ccol[:, 1:2], EPS6)
            vector.memset(ccol[:, 2:3], EPS5)
            vector.memset(ccol[:, 3:4], LN_INV128)
            vector.tensor_copy(rowsb[0:1, 0:256], gbtt[0][0:1, 0:256])
            vector.tensor_copy(rowsb[0:1, 256:264], gsc[0][0:1, 0:8])
            vector.tensor_copy(rowsb[0:1, 264:266], gsc[1][0:1, 0:2])
            vector.tensor_copy(expb[:], xhbs[0:8, 0:1024])
            vector.tensor_copy(pbcolb[:], sw[2][:, 0:2])
            vector.tensor_copy(lbcolb[:], sw[2][:, 2:4])
            vector.tensor_copy(b1eb[:, 0:128], b1c[:])
            _r = vector.reduce_sum(b1rs[:], b1c[:], axis=AX)
            S["ve"].inc(_r)
            vector.wait_ge(S["ve"].h, S["ve"].v)
            _li = vector.tensor_copy(b1eb[:, 128:129], b1rs[:])
            S["stp"].inc(_li)
        block.vector(sec_set1)

        def sec_set1gp(gpsimd):
            gpsimd.wait_ge(S["ldsm"].h, S["ldsm"].v)
            gpsimd.wait_ge(S["stp"].h, njobs)
            tkc = sw[2][:, 4:20]
            gpsimd.tensor_add(tkc, tkc, sw[3][:, 0:16])
            gpsimd.tensor_relu(tkc, tkc)
            gpsimd.tensor_single_scalar(tkc, tkc, INV128, ALU.mult)
            gpsimd.tensor_single_scalar(ccol[:, 0:1], sw[2][:, 19:20], -1.0,
                                        ALU.mult)
            _g = gpsimd.tensor_single_scalar(sw[3][:, 0:16], tkc, -1.0, ALU.mult)
            S["stp"].inc(_g)
        block.gpsimd(sec_set1gp)

        def sec_set1b(vector):
            vector.wait_ge(S["stp"].h, njobs + 2)
            vector.tensor_tensor(
                masktok[:].rearrange("p (b i) -> p b i", b=4),
                masktok[:].rearrange("p (b i) -> p b i", b=4),
                bass.AP(sw[3], 0, [[128, 128], [0, 4], [1, 16]]), ALU.mult)
            _g = vector.tensor_mul(g2scan[:], g2scan[:], g2scan[:])
            S["stp"].inc(_g)
        block.vector(sec_set1b)

        def sec_gp_id(gpsimd):
            gpsimd.wait_ge(S["ldsm"].h, S["ldsm"].v)
            gpsimd.wait_ge(S["stp"].h, njobs)
            gpsimd.memset(sw[1][:], 1.0)
            S["gp"].inc(gpsimd.affine_select(
                sw[1][:], sw[1][:], pattern=[[-1, 128]], compare_op=ALU.is_equal,
                fill=0.0, base=0, channel_multiplier=1))
        block.gpsimd(sec_gp_id)

        def sec_te_w1(tensor):
            tensor.wait_ge(S["gp"].h, S["gp"].v)
            w1i = []
            for g in range(4):
                src = gbtt[1] if (g % 2) == 0 else xhbt[0]
                w1i.append(MM(tensor, pw1[:, 128 * g:128 * (g + 1)], sw[1][:],
                              src[:, 0:128], start=(g == 0), stop=(g == 3)))
            S["te"].inc(w1i[-1])
        block.tensor(sec_te_w1)

        def sec_act_pin(scalar):
            scalar.add_instruction(mybir.InstLoadActFuncSet(
                name=nc.get_next_instruction_name(), act_func_set_id=6,
                ins=[], outs=[]))
        block.scalar(sec_act_pin)

        def sec_act_mir0(scalar):
            scalar.wait_ge(S["te"].h, S["te"].v)
            for g in range(4):
                scalar.activation(w1s[:, 128 * g:128 * (g + 1)],
                                  pw1[:, 128 * g:128 * (g + 1)], AF.Copy,
                                  accum_out=rsw1[:, g:g + 1])
            _li = scalar.activation(rsw1b[:], rsw1[:], AF.Copy)
            S["mir"].inc(_li)
        block.scalar(sec_act_mir0)

        def sec_te_rows(tensor):
            tensor.wait_ge(S["stp"].h, njobs + 3)
            for c in range(2):
                MM(tensor, put[0:1, 0:256], pbcolb[:, c:c + 1],
                              wob[:, 256 * c:256 * (c + 1)],
                              start=(c == 0), stop=(c == 1))
            rowi = []
            for c in range(2):
                for hh in range(2):
                    pu = pu0 if hh == 0 else pu1
                    rowi.append(MM(tensor, pu[0:1, 0:512],
                                  lbcolb[:, c:c + 1],
                                  upb[:, 1024 * c + 512 * hh:1024 * c + 512 * (hh + 1)],
                                  start=(c == 0), stop=(c == 1)))
            S["te"].inc(rowi[-1])
        block.tensor(sec_te_rows)

        def sec_dve_rows(vector):
            vector.wait_ge(S["te"].h, S["te"].v)
            vector.tensor_copy(rowbpw[0:1, 0:256], put[0:1, 0:256])
            vector.tensor_add(xhbs[32:33, 0:512], xhbs[32:33, 0:512], pu0[0:1, 0:512])
            vector.tensor_add(xhbs[32:33, 512:1024], xhbs[32:33, 512:1024], pu1[0:1, 0:512])
            vector.tensor_copy(rowup[0:1, 0:1024], xhbs[32:33, 0:1024])
            vector.memset(xhbs[:], 0.0)
            for c in range(2):
                vector.tensor_single_scalar(wob[:, 256 * c:256 * (c + 1)],
                                            wob[:, 256 * c:256 * (c + 1)],
                                            pwcol[:, c:c + 1], ALU.mult)
                _li = vector.tensor_single_scalar(upb[:, 1024 * c:1024 * (c + 1)],
                                            upb[:, 1024 * c:1024 * (c + 1)],
                                            lwcol[:, c:c + 1], ALU.mult)
            S["stp"].inc(_li)
        block.vector(sec_dve_rows)
        STP_DONE = S["stp"].v

        def sec_x01(sync):
            sync.wait_ge(S["stp"].h, njobs)
            for it in range(min(2, NT)):
                for s in range(2):
                    S["ldx"].inc(sync.dma_start(
                        xb[s][it % 2][:], d["x"][s, 128 * it:128 * (it + 1), :]), 16)
        block.sync(sec_x01)

        for it in range(NT):
            buf = it % 2

            for s in range(2):
                def ph_ln1_act(scalar, s=s, it=it, buf=buf):
                    scalar.wait_ge(S["ldx"].h, 32 * (it + 1))
                    scalar.wait_ge(S["sto"].h, S["sto"].v)
                    scalar.wait_ge(S["evA"].h, S["evA"].v)
                    scalar.wait_ge(S["expmm"].h, S["expmm"].v)
                    if it == 0:
                        scalar.wait_ge(S["stp"].h, STP_DONE)
                    scalar.activation(xnf[s][:], xb[s][buf][:], AF.Copy,
                                      accum_out=lnc[:, 8 * s:8 * s + 1])
                    _li = scalar.activation(xnf[s][:], xb[s][buf][:], AF.Square,
                                      accum_out=lnc[:, 8 * s + 1:8 * s + 2])
                    S["act"].inc(_li)
                block.scalar(ph_ln1_act)

                def ph_ln1_gp2(gpsimd, s=s):
                    gpsimd.wait_ge(S["act"].h, S["act"].v)
                    b0 = 8 * s
                    gpsimd.tensor_single_scalar(lnc[:, b0 + 2:b0 + 3],
                                                lnc[:, b0:b0 + 1], 1.0 / DM, ALU.mult)
                    gpsimd.tensor_single_scalar(lnc[:, b0 + 3:b0 + 4],
                                                lnc[:, b0 + 1:b0 + 2], 1.0 / DM, ALU.mult)
                    gpsimd.tensor_mul(lnc[:, b0 + 4:b0 + 5], lnc[:, b0 + 2:b0 + 3],
                                      lnc[:, b0 + 2:b0 + 3])
                    _li = gpsimd.tensor_sub(lnc[:, b0 + 5:b0 + 6], lnc[:, b0 + 3:b0 + 4],
                                      lnc[:, b0 + 4:b0 + 5])
                    S["gp"].inc(_li)
                block.gpsimd(ph_ln1_gp2)

                def ph_ln1_act2(scalar, s=s):
                    scalar.wait_ge(S["gp"].h, S["gp"].v)
                    b0 = 8 * s
                    scalar.activation(lnc[:, b0 + 5:b0 + 6], lnc[:, b0 + 5:b0 + 6],
                                      AF.Ln, bias=ccol[:, 2:3])
                    _li = scalar.activation(lnc[:, b0 + 6:b0 + 7], lnc[:, b0 + 5:b0 + 6],
                                      AF.Exp, scale=-0.5)
                    S["act"].inc(_li)
                block.scalar(ph_ln1_act2)

                def ph_ln1_dve(vector, s=s, buf=buf):
                    vector.wait_ge(S["act"].h, S["act"].v)
                    b0 = 8 * s
                    _li = vector.tensor_scalar(xnb[s][:], xb[s][buf][:],
                                         lnc[:, b0 + 2:b0 + 3], lnc[:, b0 + 6:b0 + 7],
                                         ALU.subtract, ALU.mult)
                    S["dve"].inc(_li)
                block.vector(ph_ln1_dve)

            for s in range(2):
                for grp in range(2):
                    pbank = put_bf if grp == 0 else pe1_bf

                    def ph_txn_te(tensor, s=s, grp=grp, pbank=pbank):
                        tensor.wait_ge(S["dve"].h, S["dve"].v)
                        tensor.wait_ge(S["comb"].h, S["comb"].v)
                        tensor.wait_ge(S["act"].h, S["act"].v)
                        tensor.wait_ge(S["evA"].h, S["evA"].v)
                        ti = []
                        for j in range(4):
                            c = grp * 4 + j
                            ti.append(tensor.transpose(
                                pbank[:, 128 * j:128 * (j + 1)],
                                xnb[s][:, 128 * c:128 * (c + 1)], idb[:]))
                        S["te"].inc(ti[-1])
                    block.tensor(ph_txn_te)

                    def ph_txn_ev(vector, s=s, grp=grp, pbank=pbank):
                        vector.wait_ge(S["te"].h, S["te"].v)
                        vector.wait_ge(S["expmm"].h, S["expmm"].v)
                        for j in range(4):
                            c = grp * 4 + j
                            _li = vector.tensor_copy(xnf[s][:, 128 * c:128 * (c + 1)],
                                               pbank[:, 128 * j:128 * (j + 1)])
                        S["evA"].inc(_li)
                    block.vector(ph_txn_ev)

            for s in range(2):
                def ph_down_te(tensor, s=s):
                    tensor.wait_ge(S["evA"].h, S["evA"].v)
                    tensor.wait_ge(S["comb"].h, S["comb"].v)
                    tensor.wait_ge(S["act"].h, S["act"].v)
                    tensor.wait_ge(S["dve"].h, S["dve"].v)
                    for c in range(8):
                        MM(tensor, put[:, 0:256], xnf[s][:, 128 * c:128 * (c + 1)],
                                      dwb[:, 256 * c:256 * (c + 1)],
                                      start=(c == 0), stop=False)
                    MM(tensor, put[:, 0:256], ones1[:], rowsb[0:1, 0:256],
                                  start=False, stop=True)
                    for c in range(8):
                        MM(tensor, pe0[:, 256:264], xnf[s][:, 128 * c:128 * (c + 1)],
                                      gwb[:, 8 * c:8 * (c + 1)],
                                      start=(c == 0), stop=False)
                    _li = MM(tensor, pe0[:, 256:264], ones1[:], rowsb[0:1, 256:264],
                                  start=False, stop=True)
                    S["te"].inc(_li)
                block.tensor(ph_down_te)

                def ph_down_act(scalar, s=s):
                    scalar.wait_ge(S["te"].h, S["te"].v)
                    _li = scalar.activation(xvt[s][:], put[:, 0:256], AF.Exp, scale=-1.0)
                    S["act"].inc(_li)
                block.scalar(ph_down_act)

                def ph_down_dve(vector, s=s):
                    vector.wait_ge(S["act"].h, S["act"].v)
                    vector.wait_ge(S["w8s"].h, S["w8s"].v)
                    vector.tensor_single_scalar(xvt[s][:], xvt[s][:], 1.0, ALU.add)
                    vector.reciprocal(xqt[s][:], xvt[s][:])
                    vector.tensor_tensor(htb[s][:], put[:, 0:256], xqt[s][:], ALU.mult)
                    _g = vector.tensor_copy(glog[s][:], pe0[:, 256:264])
                    S["ve"].inc(_g)
                    _li = vector.tensor_copy(htb[s][:, 0:8], htb[s][:, 0:8])
                    S["dve"].inc(_li)
                block.vector(ph_down_dve)

                def ph_gate_dve1(vector, s=s):
                    vector.wait_ge(S["dve"].h, S["dve"].v)
                    vector.wait_ge(S["ve"].h, S["ve"].v)
                    g = glog[s]; sc = gsc[s]
                    _e = vector.reduce_max(sc[:, 0:1], g[:], axis=AX)
                    S["ve"].inc(_e)
                    vector.wait_ge(S["ve"].h, S["ve"].v)
                    _d = vector.tensor_single_scalar(sc[:, 4:5], sc[:, 0:1],
                                                     -1.0, ALU.mult)
                    _e = vector.tensor_tensor(sc[:, 24:32], g[:], bc(sc[:, 0:1], 8),
                                              ALU.is_ge)
                    S["ve"].inc(_e)
                    vector.wait_ge(S["ve"].h, S["ve"].v)
                    _e = vector.scalar_tensor_tensor(sc[:, 16:24], sc[:, 24:32],
                                                     -BIG, g[:], ALU.mult, ALU.add)
                    S["ve"].inc(_e)
                    vector.wait_ge(S["ve"].h, S["ve"].v)
                    _e = vector.reduce_max(sc[:, 3:4], sc[:, 16:24], axis=AX)
                    S["ve"].inc(_e)
                    vector.wait_ge(S["ve"].h, S["ve"].v)
                    _li = vector.tensor_tensor(sc[:, 20:24] if False else sc[:, 24:32],
                                               g[:], bc(sc[:, 3:4], 8),
                                               ALU.is_ge)
                    S["dve"].inc(_li)
                block.vector(ph_gate_dve1)

                def ph_gate_act(scalar, s=s):
                    scalar.wait_ge(S["dve"].h, S["dve"].v)
                    _li = scalar.activation(gsc[s][:, 8:16], glog[s][:], AF.Exp,
                                            bias=gsc[s][:, 4:5],
                                            accum_out=gsc[s][:, 1:2])
                    S["act"].inc(_li)
                block.scalar(ph_gate_act)

                def ph_gate_dve2(vector, s=s):
                    vector.wait_ge(S["act"].h, S["act"].v)
                    vector.wait_ge(S["comb"].h, S["comb"].v)
                    sc = gsc[s]
                    _e = vector.reciprocal(sc[:, 2:3], sc[:, 1:2])
                    S["ve"].inc(_e)
                    _e = vector.tensor_mul(sc[:, 8:16], sc[:, 8:16], sc[:, 24:32])
                    S["ve"].inc(_e)
                    vector.wait_ge(S["ve"].h, S["ve"].v)
                    _e = vector.tensor_tensor(w8[s][:], sc[:, 8:16],
                                              bc(sc[:, 2:3], 8), ALU.mult)
                    S["ve"].inc(_e)
                    vector.wait_ge(S["ve"].h, S["ve"].v)
                    _li = vector.tensor_copy(w8bf[s][:], w8[s][:])
                    S["w8s"].inc(_li)
                block.vector(ph_gate_dve2)

            for s in range(2):
                def ph_th_te(tensor, s=s):
                    tensor.wait_ge(S["dve"].h, S["dve"].v)
                    ti = []
                    for c in range(2):
                        ti.append(tensor.transpose(
                            pe1_bf[:, 128 * c:128 * (c + 1)],
                            htb[s][:, 128 * c:128 * (c + 1)], idb[:]))
                    S["te"].inc(ti[-1])
                block.tensor(ph_th_te)

                def ph_th_ev(vector, s=s):
                    vector.wait_ge(S["te"].h, S["te"].v)
                    for c in range(2):
                        _li = vector.tensor_copy(hfm[s][:, 128 * c:128 * (c + 1)],
                                           pe1_bf[:, 128 * c:128 * (c + 1)])
                    S["evB"].inc(_li)
                block.vector(ph_th_ev)

                def ph_qkv_te(tensor, s=s):
                    tensor.wait_ge(S["evB"].h, S["evB"].v)
                    tensor.wait_ge(S["act"].h, S["act"].v)
                    tensor.wait_ge(S["dve"].h, S["dve"].v)
                    for c in range(2):
                        MM(tensor, put[:, 0:256], hfm[s][:, 128 * c:128 * (c + 1)],
                                      wqb[:, 256 * c:256 * (c + 1)],
                                      start=(c == 0), stop=(c == 1))
                        MM(tensor, put[:, 256:512], hfm[s][:, 128 * c:128 * (c + 1)],
                                      wkb[:, 256 * c:256 * (c + 1)],
                                      start=False, stop=False)
                        MM(tensor, pe0[:, 0:256], hfm[s][:, 128 * c:128 * (c + 1)],
                                      wvb[:, 256 * c:256 * (c + 1)],
                                      start=(c == 0), stop=False)
                        MM(tensor, pe0[:, 448:450], hfm[s][:, 128 * c:128 * (c + 1)],
                                      lrwb[:, 2 * c:2 * (c + 1)], start=False, stop=False)
                    _li = MM(tensor, pe0[:, 448:450], ones1[:], rowsb[0:1, 264:266],
                                  start=False, stop=True)
                    S["te"].inc(_li)
                block.tensor(ph_qkv_te)

                def ph_qkv_ev(vector, s=s):
                    vector.wait_ge(S["te"].h, S["te"].v)
                    vector.wait_ge(S["resh"].h, S["resh"].v)
                    vector.tensor_copy(xqt[s][:], put[:, 0:256])
                    vector.tensor_copy(xkt[s][:], put[:, 256:512])
                    vector.tensor_copy(xvt[s][:], pe0[:, 0:256])
                    vector.tensor_copy(xqbt[s][:], put[:, 0:256])
                    _li = vector.tensor_copy(xkbt[s][:], put[:, 256:512])
                    S["dve"].inc(_li)
                block.vector(ph_qkv_ev)

                def ph_qkv_act(scalar, s=s):
                    scalar.wait_ge(S["te"].h, S["te"].v)
                    scalar.wait_ge(S["resh"].h, S["resh"].v)
                    _li = scalar.activation(sgm[s][:, 2:4], pe0[:, 448:450], AF.Exp,
                                      scale=-1.0)
                    S["act"].inc(_li)
                block.scalar(ph_qkv_act)

                def ph_sgm_gp(gpsimd, s=s):
                    gpsimd.wait_ge(S["act"].h, S["act"].v)
                    _li = gpsimd.tensor_single_scalar(sgm[s][:, 2:4],
                                                      sgm[s][:, 2:4], 1.0, ALU.add)
                    S["gp"].inc(_li)
                block.gpsimd(ph_sgm_gp)

                def ph_sgm_dve(vector, s=s):
                    vector.wait_ge(S["gp"].h, S["gp"].v)
                    _li = vector.reciprocal(sgm[s][:, 0:2], sgm[s][:, 2:4])
                    S["dve"].inc(_li)
                block.vector(ph_sgm_dve)

                def ph_tkq_te(tensor, s=s):
                    tensor.wait_ge(S["dve"].h, S["dve"].v)
                    ti = []
                    for hh in range(2):
                        ti.append(tensor.transpose(
                            pe1_bf[:, 128 * hh:128 * (hh + 1)],
                            xkbt[s][:, 128 * hh:128 * (hh + 1)], idb[:]))
                    for hh in range(2):
                        ti.append(tensor.transpose(
                            pe1_bf[:, 256 + 128 * hh:256 + 128 * (hh + 1)],
                            xqbt[s][:, 128 * hh:128 * (hh + 1)], idb[:]))
                    S["te"].inc(ti[-1])
                block.tensor(ph_tkq_te)

                def ph_tkq_ev(vector, s=s):
                    vector.wait_ge(S["te"].h, S["te"].v)
                    vector.wait_ge(S["z1"].h, S["z1"].v)
                    vector.wait_ge(S["sp"].h, S["sp"].v)
                    vector.wait_ge(S["phU"].h, S["phU"].v)
                    for hh in range(2):
                        dstk = kqt[s][:, 256 * hh:256 * (hh + 1)].rearrange(
                            "p (n kq j) -> p n kq j", n=8, kq=2)[:, :, 0, :]
                        dstq = kqt[s][:, 256 * hh:256 * (hh + 1)].rearrange(
                            "p (n kq j) -> p n kq j", n=8, kq=2)[:, :, 1, :]
                        vector.tensor_copy(
                            dstk, pe1_bf[:, 128 * hh:128 * (hh + 1)].rearrange(
                                "p (n j) -> p n j", n=8))
                        _li = vector.tensor_copy(
                            dstq, pe1_bf[:, 256 + 128 * hh:256 + 128 * (hh + 1)].rearrange(
                                "p (n j) -> p n j", n=8))
                    S["evB"].inc(_li)
                block.vector(ph_tkq_ev)

                def ph_gbt(vector, s=s):
                    vector.wait_ge(S["resh"].h, S["resh"].v)
                    vector.tensor_sub(ows[s][:], xvt[s][:], xkt[s][:])
                    vector.tensor_sub(ows[s][:], b_rep[:], ows[s][:])
                    _li = vector.tensor_mul(gbtt[s][:], ows[s][:], gw_rep[:])
                    S["dve"].inc(_li)
                block.vector(ph_gbt)

            def mk_resh(eng, sel, it=it):
                eng.wait_ge(S["dve"].h, S["dve"].v)
                eng.wait_ge(S["act"].h, S["act"].v)
                if it > 0:
                    eng.wait_ge(S["gsn"].h, S["gsn"].v)
                    eng.wait_ge(S["z2m"].h, S["z2m"].v)
                k = 0
                for s in range(2):
                    for hh in range(2):
                        g = 2 * s + hh
                        for n in range(NSTEP):
                            for dst, srcv in (
                                (xks[32 * g:32 * g + 16, 128 * n:128 * (n + 1)],
                                 xkbt[s][16 * n:16 * n + 16, 128 * hh:128 * (hh + 1)]),
                                (gbts[32 * g:32 * g + 16, 128 * n:128 * (n + 1)],
                                 gbtt[s][16 * n:16 * n + 16, 128 * hh:128 * (hh + 1)]),
                                (sgs[32 * g:32 * g + 16, n:n + 1],
                                 sgm[s][16 * n:16 * n + 16, hh:hh + 1]),
                            ):
                                if k % 3 == sel:
                                    S["resh"].inc(eng.dma_start(dst, srcv), 16)
                                k += 1

            def ph_resh0(sync, it=it):
                mk_resh(sync, 0, it)
            block.sync(ph_resh0)

            def ph_resh1(gpsimd, it=it):
                mk_resh(gpsimd, 1, it)
            block.gpsimd(ph_resh1)

            def ph_resh2(scalar, it=it):
                mk_resh(scalar, 2, it)
            block.scalar(ph_resh2)

            def ph_resh_fin(sync):
                sync.wait_ge(S["resh"].h, S["resh"].v)
                S["resh"].inc(sync.nop())
            block.sync(ph_resh_fin)

            def ph_negsig(gpsimd):
                gpsimd.wait_ge(S["resh"].h, S["resh"].v)
                _li = gpsimd.tensor_single_scalar(sgs[:, 8:16], sgs[:, 0:8], -1.0, ALU.mult)
                S["gp"].inc(_li)
            block.gpsimd(ph_negsig)

            def ph_sp0(tensor, it=it):
                tensor.wait_ge(S["evB"].h, S["evB"].v)
                if it > 0:
                    tensor.wait_ge(S["msk"].h, S["msk"].v)
                for g in range(4):
                    s, hh = divmod(g, 2)
                    S["sp"].inc(MM(
                        tensor, psp[32 * g:32 * g + 16, 16 * g:16 * g + 16],
                        kqt[s][:, 256 * hh:256 * hh + 16],
                        kqt[s][:, 256 * hh + 16:256 * hh + 32],
                        start=True, stop=True,
                        tile_position=tpos(0, 32 * g)))
            block.tensor(ph_sp0)

            def emit_expgroup(gi, buf=buf):
                pass_gi = gi
                s_, rem = divmod(gi, 16)
                hh_, e_ = divmod(rem, 8)
                bank = pe0 if (e_ % 2 == 0) else pe1

                def ph_exp_mm(tensor, s=s_, e=e_, hh=hh_, bank=bank, gi=gi):
                    if S["comb"].v >= 2:
                        tensor.wait_ge(S["comb"].h, S["comb"].v - 1)
                    if gi == 0:
                        tensor.wait_ge(S["dve"].h, S["dve"].v)
                        tensor.wait_ge(S["act"].h, S["act"].v)
                    if e == 0 and hh == 0:
                        tensor.wait_ge(S["w8s"].h, S["w8s"].v)
                    for c in range(8):
                        _li = MM(tensor, bank[:, 0:512],
                                 xnf[s][:, 128 * c:128 * (c + 1)],
                                 ewb[:, (c * 8 + e) * 1024 + 512 * hh:
                                     (c * 8 + e) * 1024 + 512 * (hh + 1)],
                                 start=(c == 0), stop=(c == 7))
                    S["expmm"].inc(_li)
                block.tensor(ph_exp_mm)

                def ph_comb(vector, s=s_, e=e_, hh=hh_, bank=bank, buf=buf):
                    vector.wait_ge(S["expmm"].h, S["expmm"].v)
                    if e == 0:
                        vector.wait_ge(S["w8s"].h, S["w8s"].v)
                    dst = xb[s][buf][:, 512 * hh:512 * (hh + 1)]
                    _li = vector.scalar_tensor_tensor(dst, bank[:, 0:512],
                                                      w8[s][:, e:e + 1], dst,
                                                      ALU.mult, ALU.add)
                    S["comb"].inc(_li)
                block.vector(ph_comb)

            for n in range(NSTEP):
                N = NSTEP * it + n

                def sc_mask(vector, N=N):
                    if "mask" in NEUTER:
                        vector.wait_ge(S["sp"].h, S["sp"].v)
                        _d = vector.memset(s4t[:, 0:128], 0.0)
                        S["msk"].inc(_d)
                        return
                    vector.wait_ge(S["sp"].h, S["sp"].v)
                    _li = vector.scalar_tensor_tensor(
                        s4t[:].rearrange("p (b c) -> p b c", c=32)[:, :, 16:32],
                        psp[:, 0:64].rearrange("p (b i) -> p b i", b=4),
                        1.0,
                        masktok[:].rearrange("p (b i) -> p b i", b=4),
                        ALU.add, ALU.mult)
                    S["msk"].inc(_li)
                if "mask" in SCEMIT:
                    block.vector(sc_mask)

                def sc_mma(tensor, N=N, n=n, it=it):
                    tensor.wait_ge(S["mir"].h, S["mir"].v)
                    if n == 0:
                        tensor.wait_ge(S["gp"].h, S["gp"].v)
                        tensor.wait_ge(S["resh"].h, S["resh"].v)
                        tensor.wait_ge(S["xus"].h, S["xus"].v)
                    tensor.wait_ge(S["xhb"].h, S["xhb"].v)
                    tensor.wait_ge(S["st2"].h, S["st2"].v)
                    S["z1"].inc(MM(tensor, pzc[:, 0:129], indb[:],
                                   b1eb[:, 0:129], start=True, stop=False))
                    for g in range(4):
                        s, hh = divmod(g, 2)
                        lhs = kqt[s][:, 256 * hh + 32 * n:256 * hh + 32 * n + 32]
                        S["z1"].inc(MM(
                            tensor, pzc[32 * g:32 * (g + 1), 0:128], lhs,
                            w1s[:, 128 * g:128 * (g + 1)],
                            start=False, stop=False,
                            tile_position=tpos(0, 32 * g)))
                        S["z1"].inc(MM(
                            tensor, pzc[32 * g:32 * (g + 1), 128:129], lhs,
                            rsw1b[:, g:g + 1],
                            start=False, stop=(g == 3),
                            tile_position=tpos(0, 32 * g)))
                if "mma" in SCEMIT:
                    block.tensor(sc_mma)

                def sc_st1x(vector, N=N):
                    vector.wait_ge(S["z1"].h, S["z1"].v)
                    _li = vector.tensor_copy(scols[:, 15:16], pzc[:, 128:129])
                    S["sx1"].inc(_li)
                block.vector(sc_st1x)

                def sc_gpb(gpsimd, N=N):
                    gpsimd.wait_ge(S["sx1"].h, S["sx1"].v)
                    gpsimd.tensor_single_scalar(scols[:, 1:2], scols[:, 15:16],
                                                INV128, ALU.mult)
                    gpsimd.tensor_mul(scols[:, 3:4], scols[:, 1:2], scols[:, 1:2])
                    gpsimd.tensor_single_scalar(scols[:, 2:3], scols[:, 3:4],
                                                -1.0, ALU.mult)
                    _li = gpsimd.tensor_single_scalar(scols[:, 2:3], scols[:, 2:3],
                                                      EPS6, ALU.add)
                    S["gpb"].inc(_li)
                block.gpsimd(sc_gpb)

                def sc_act1(scalar, N=N):
                    scalar.wait_ge(S["z1"].h, S["z1"].v)
                    _li = scalar.activation(sw[0][:], pzc[:, 0:128], AF.Square,
                                            accum_out=scols[:, 0:1])
                    S["st1a"].inc(_li)
                    scalar.wait_ge(S["st1a"].h, S["st1a"].v)
                    scalar.wait_ge(S["gpb"].h, S["gpb"].v)
                    scalar.activation(scols[:, 5:6], scols[:, 0:1], AF.Ln,
                                      scale=INV128, bias=scols[:, 2:3])
                    scalar.activation(scols[:, 6:7], scols[:, 5:6], AF.Exp,
                                      scale=-0.5)
                    _li = scalar.activation(scols[:, 9:10], scols[:, 5:6], AF.Exp,
                                            scale=-0.5, bias=ccol[:, 3:4])
                    S["std1"].inc(_li)
                block.scalar(sc_act1)

                def sc_chainA(vector, N=N, n=n):
                    vector.wait_ge(S["std1"].h, S["std1"].v)
                    vector.tensor_tensor(sw[1][:], pzc[:, 0:128],
                                         bc(scols[:, 1:2], 128), ALU.subtract)
                    vector.tensor_tensor(sw[2][:], sw[1][:],
                                         bc(scols[:, 6:7], 128), ALU.mult)
                    vector.tensor_mul(sw[3][:], sw[2][:], g2scan[:])
                    vector.scalar_tensor_tensor(sw[4][:], sw[3][:], 0.0,
                                                gbts[:, 128 * n:128 * (n + 1)],
                                                ALU.add, ALU.add,
                                                accum_out=c12[:, 0:1])
                    _li = vector.scalar_tensor_tensor(sw[0][:], sw[4][:], 1.0,
                                                sw[2][:], ALU.mult, ALU.mult,
                                                accum_out=c12[:, 1:2])
                    S["c12s"].inc(_li)
                    vector.tensor_tensor(sw[1][:], sw[4][:],
                                         bc(scols[:, 6:7], 128), ALU.mult)
                    vector.wait_ge(S["qrr"].h, S["qrr"].v + 1)
                    vector.tensor_tensor(sw[3][:], sw[2][:],
                                         bc(scols[:, 8:9], 128), ALU.mult)
                    vector.tensor_tensor(sw[4][:], sw[3][:],
                                         bc(scols[:, 7:8], 128), ALU.add)
                    vector.tensor_sub(sw[3][:], sw[4][:], sw[1][:])
                    vector.tensor_tensor(gsn[:], sw[3][:],
                                         bc(sgs[:, 8 + n:9 + n], 128), ALU.mult)
                    _li = vector.tensor_tensor(tsg[:], gsn[:],
                                         bc(ccol[:, 0:1], 128), ALU.mult)
                    S["gsn"].inc(_li)
                block.vector(sc_chainA)

                def sc_qrr(gpsimd, N=N):
                    gpsimd.wait_ge(S["c12s"].h, S["c12s"].v)
                    gpsimd.wait_ge(S["std1"].h, S["std1"].v)
                    gpsimd.tensor_single_scalar(scols[:, 7:8], c12[:, 0:1],
                                                INV128, ALU.mult)
                    gpsimd.tensor_mul(scols[:, 7:8], scols[:, 7:8], scols[:, 6:7])
                    gpsimd.tensor_single_scalar(scols[:, 8:9], c12[:, 1:2],
                                                INV128, ALU.mult)
                    _li = gpsimd.tensor_mul(scols[:, 8:9], scols[:, 8:9],
                                            scols[:, 6:7])
                    S["qrr"].inc(_li)
                block.gpsimd(sc_qrr)

                def sc_mm2(tensor, N=N, n=n):
                    if "mm2" in NEUTER:
                        tensor.wait_ge(S["gsn"].h, S["gsn"].v)
                        tensor.wait_ge(S["msk"].h, S["msk"].v)
                        S["z2m"].inc(nc.tensor.nop())
                        return
                    tensor.wait_ge(S["gsn"].h, S["gsn"].v)
                    tensor.wait_ge(S["msk"].h, S["msk"].v)
                    def w1mm(g, stop):
                        return MM(tensor, pw1[:, 128 * g:128 * (g + 1)],
                                  xks[32 * g:32 * g + 16, 128 * n:128 * (n + 1)],
                                  tsg[32 * g:32 * g + 16, :],
                                  start=False, stop=stop,
                                  tile_position=tpos(32 * g, 0))
                    S["z2m"].inc(MM(tensor, pzc[:, 0:128], s4t[:], gsn[:],
                                    start=False, stop=False))
                    S["z2m"].inc(w1mm(0, False))
                    S["z2m"].inc(MM(tensor, pzc[0:4, 384:512], ind2b[:], tsg[:],
                                    start=True, stop=True))
                    S["z2m"].inc(w1mm(1, False))
                    MM(tensor, pzc[0:4, 448:512], zb4[:], tsg[:, 0:64],
                       start=False, stop=False)
                    S["z2m"].inc(w1mm(2, False))
                    MM(tensor, pzc[0:4, 448:512], zb4[:], tsg[:, 0:64],
                       start=False, stop=True)
                    S["z2m"].inc(w1mm(3, True))
                if "mm2" in SCEMIT:
                    block.tensor(sc_mm2)

                if n + 1 < NSTEP:
                    def sc_spn(tensor, N=N, n=n):
                        if "spn" in NEUTER:
                            tensor.wait_ge(S["msk"].h, S["msk"].v)
                            S["sp"].inc(nc.tensor.nop())
                            return
                        tensor.wait_ge(S["msk"].h, S["msk"].v)
                        for g in range(4):
                            s, hh = divmod(g, 2)
                            S["sp"].inc(MM(
                                tensor,
                                psp[32 * g:32 * g + 16, 16 * g:16 * g + 16],
                                kqt[s][:, 256 * hh + 32 * (n + 1):
                                       256 * hh + 32 * (n + 1) + 16],
                                kqt[s][:, 256 * hh + 32 * (n + 1) + 16:
                                       256 * hh + 32 * (n + 1) + 32],
                                start=True, stop=True,
                                tile_position=tpos(0, 32 * g)))
                    if "spn" in SCEMIT:
                        block.tensor(sc_spn)

                def sc_act2(scalar, N=N):
                    scalar.wait_ge(S["z2m"].h, S["z2m"].v)
                    _li = scalar.activation(sw[0][:], pzc[:, 0:128], AF.Square,
                                      accum_out=scols[:, 10:11])
                    S["st2"].inc(_li)
                    for g in range(4):
                        scalar.activation(w1s[:, 128 * g:128 * (g + 1)],
                                          pw1[:, 128 * g:128 * (g + 1)], AF.Copy,
                                          accum_out=rsw1[:, g:g + 1])
                    _li = scalar.activation(rsw1b[:], rsw1[:], AF.Copy)
                    S["mir"].inc(_li)
                block.scalar(sc_act2)

                def sc_st2d(gpsimd, N=N):
                    gpsimd.wait_ge(S["st2"].h, S["st2"].v)
                    gpsimd.tensor_single_scalar(scols[:, 11:12], scols[:, 10:11],
                                                INV128, ALU.mult)
                    gpsimd.tensor_sub(scols[:, 11:12], scols[:, 11:12],
                                      scols[:, 3:4])
                    _li = gpsimd.tensor_single_scalar(scols[:, 11:12],
                                                scols[:, 11:12], EPS6, ALU.add)
                    S["var2"].inc(_li)
                block.gpsimd(sc_st2d)

                def sc_rstd2(scalar, N=N):
                    scalar.wait_ge(S["var2"].h, S["var2"].v)
                    scalar.activation(scols[:, 12:13], scols[:, 11:12], AF.Ln)
                    _li = scalar.activation(scols[:, 13:14], scols[:, 12:13], AF.Exp,
                                      scale=-0.5)
                    S["std2"].inc(_li)
                block.scalar(sc_rstd2)

                def sc_xhb(vector, N=N, n=n):
                    vector.wait_ge(S["std2"].h, S["std2"].v)
                    vector.tensor_tensor(sw[1][:], pzc[:, 0:128],
                                         bc(scols[:, 1:2], 128), ALU.subtract)
                    vector.tensor_tensor(xhbs[:, 128 * n:128 * (n + 1)], sw[1][:],
                                         bc(scols[:, 13:14], 128), ALU.mult)
                    vector.tensor_add(b1c[:], b1c[:], pzc[0:4, 384:512])
                    vector.reduce_sum(b1rs[:], b1c[:], axis=AX)
                    vector.tensor_copy(b1eb[:, 0:128], b1c[:])
                    _li = vector.tensor_copy(b1eb[:, 128:129], b1rs[:])
                    S["xhb"].inc(_li)
                block.vector(sc_xhb)

                for gi in range(4 * n, 4 * n + 4):
                    emit_expgroup(gi)

            def mk_xout(eng, sel, it=it):
                eng.wait_ge(S["xhb"].h, S["xhb"].v)
                eng.wait_ge(S["dve"].h, S["dve"].v)
                k = 0
                for s in range(2):
                    for hh in range(2):
                        g = 2 * s + hh
                        for n in range(NSTEP):
                            if k % 2 == sel:
                                S["xus"].inc(eng.dma_start(
                                    xhbt[s][16 * n:16 * n + 16, 128 * hh:128 * (hh + 1)],
                                    xhbs[32 * g + 16:32 * g + 32, 128 * n:128 * (n + 1)]), 16)
                            k += 1

            def ph_xout0(sync, it=it):
                mk_xout(sync, 0, it)
            block.sync(ph_xout0)

            def ph_xout1(gpsimd, it=it):
                mk_xout(gpsimd, 1, it)
            block.gpsimd(ph_xout1)

            def ph_xout_fin(sync):
                sync.wait_ge(S["xus"].h, S["xus"].v)
                S["xus"].inc(sync.nop())
            block.sync(ph_xout_fin)

            for s in range(2):
                def ph_outs_dve(vector, s=s):
                    vector.wait_ge(S["xus"].h, S["xus"].v)
                    vector.tensor_mul(ows[s][:], xhbt[s][:], gw_rep[:])
                    vector.tensor_add(ows[s][:], ows[s][:], b_rep[:])
                    _li = vector.tensor_add(ows[s][:], ows[s][:], xqt[s][:])
                    S["dve"].inc(_li)
                block.vector(ph_outs_dve)

                def ph_lnp_act(scalar, s=s):
                    scalar.wait_ge(S["dve"].h, S["dve"].v)
                    b0 = 16 + 8 * s
                    scalar.activation(gbtt[s][:], ows[s][:], AF.Copy,
                                      accum_out=lnc[:, b0:b0 + 1])
                    _li = scalar.activation(gbtt[s][:], ows[s][:], AF.Square,
                                      accum_out=lnc[:, b0 + 1:b0 + 2])
                    S["act"].inc(_li)
                block.scalar(ph_lnp_act)

                def ph_lnp_gp2(gpsimd, s=s):
                    gpsimd.wait_ge(S["act"].h, S["act"].v)
                    b0 = 16 + 8 * s
                    gpsimd.tensor_single_scalar(lnc[:, b0 + 2:b0 + 3],
                                                lnc[:, b0:b0 + 1], 1.0 / DD, ALU.mult)
                    gpsimd.tensor_single_scalar(lnc[:, b0 + 3:b0 + 4],
                                                lnc[:, b0 + 1:b0 + 2], 1.0 / DD,
                                                ALU.mult)
                    gpsimd.tensor_mul(lnc[:, b0 + 4:b0 + 5], lnc[:, b0 + 2:b0 + 3],
                                      lnc[:, b0 + 2:b0 + 3])
                    _li = gpsimd.tensor_sub(lnc[:, b0 + 5:b0 + 6], lnc[:, b0 + 3:b0 + 4],
                                      lnc[:, b0 + 4:b0 + 5])
                    S["gp"].inc(_li)
                block.gpsimd(ph_lnp_gp2)

                def ph_lnp_act2(scalar, s=s):
                    scalar.wait_ge(S["gp"].h, S["gp"].v)
                    b0 = 16 + 8 * s
                    scalar.activation(lnc[:, b0 + 5:b0 + 6], lnc[:, b0 + 5:b0 + 6],
                                      AF.Ln, bias=ccol[:, 2:3])
                    _li = scalar.activation(lnc[:, b0 + 6:b0 + 7], lnc[:, b0 + 5:b0 + 6],
                                      AF.Exp, scale=-0.5)
                    S["act"].inc(_li)
                block.scalar(ph_lnp_act2)

                def ph_lnp_dve(vector, s=s):
                    vector.wait_ge(S["act"].h, S["act"].v)
                    b0 = 16 + 8 * s
                    _li = vector.tensor_scalar(kqt[s][:, 256:512], ows[s][:],
                                         lnc[:, b0 + 2:b0 + 3], lnc[:, b0 + 6:b0 + 7],
                                         ALU.subtract, ALU.mult)
                    S["dve"].inc(_li)
                block.vector(ph_lnp_dve)

                def ph_wo_te(tensor, s=s):
                    tensor.wait_ge(S["dve"].h, S["dve"].v)
                    for c in range(2):
                        _li = tensor.transpose(put_bf[:, 512 + 128 * c:512 + 128 * (c + 1)],
                                         kqt[s][:, 256 + 128 * c:256 + 128 * (c + 1)], idb[:])
                    S["te"].inc(_li)
                block.tensor(ph_wo_te)

                def ph_wo_ev(vector, s=s):
                    vector.wait_ge(S["te"].h, S["te"].v)
                    for c in range(2):
                        _li = vector.tensor_copy(kqt[s][:, 128 * c:128 * (c + 1)],
                                           put_bf[:, 512 + 128 * c:512 + 128 * (c + 1)])
                    S["evB"].inc(_li)
                block.vector(ph_wo_ev)

                def ph_wo_mm(tensor, s=s):
                    tensor.wait_ge(S["evB"].h, S["evB"].v)
                    tensor.wait_ge(S["act"].h, S["act"].v)
                    tensor.wait_ge(S["dve"].h, S["dve"].v)
                    for c in range(2):
                        MM(tensor, put[:, 0:256], kqt[s][:, 128 * c:128 * (c + 1)],
                                      wob[:, 256 * c:256 * (c + 1)],
                                      start=(c == 0), stop=False)
                    _li = MM(tensor, put[:, 0:256], ones1[:], rowbpw[0:1, 0:256],
                                  start=False, stop=True)
                    S["te"].inc(_li)
                block.tensor(ph_wo_mm)

                def ph_ln2_act(scalar, s=s):
                    scalar.wait_ge(S["te"].h, S["te"].v)
                    b0 = 32 + 8 * s
                    scalar.activation(ows[s][:], put[:, 0:256], AF.Copy,
                                      accum_out=lnc[:, b0:b0 + 1])
                    _li = scalar.activation(ows[s][:], put[:, 0:256], AF.Square,
                                      accum_out=lnc[:, b0 + 1:b0 + 2])
                    S["act"].inc(_li)
                block.scalar(ph_ln2_act)

                def ph_ln2_gp(gpsimd, s=s):
                    gpsimd.wait_ge(S["act"].h, S["act"].v)
                    b0 = 32 + 8 * s
                    gpsimd.tensor_single_scalar(lnc[:, b0 + 2:b0 + 3],
                                                lnc[:, b0:b0 + 1], 1.0 / DD, ALU.mult)
                    gpsimd.tensor_single_scalar(lnc[:, b0 + 3:b0 + 4],
                                                lnc[:, b0 + 1:b0 + 2], 1.0 / DD,
                                                ALU.mult)
                    gpsimd.tensor_mul(lnc[:, b0 + 4:b0 + 5], lnc[:, b0 + 2:b0 + 3],
                                      lnc[:, b0 + 2:b0 + 3])
                    _li = gpsimd.tensor_sub(lnc[:, b0 + 5:b0 + 6], lnc[:, b0 + 3:b0 + 4],
                                      lnc[:, b0 + 4:b0 + 5])
                    S["gp"].inc(_li)
                block.gpsimd(ph_ln2_gp)

                def ph_ln2_act2(scalar, s=s):
                    scalar.wait_ge(S["gp"].h, S["gp"].v)
                    b0 = 32 + 8 * s
                    scalar.activation(lnc[:, b0 + 5:b0 + 6], lnc[:, b0 + 5:b0 + 6],
                                      AF.Ln, bias=ccol[:, 2:3])
                    _li = scalar.activation(lnc[:, b0 + 6:b0 + 7], lnc[:, b0 + 5:b0 + 6],
                                      AF.Exp, scale=-0.5)
                    S["act"].inc(_li)
                block.scalar(ph_ln2_act2)

                def ph_ln2_dve(vector, s=s):
                    vector.wait_ge(S["act"].h, S["act"].v)
                    b0 = 32 + 8 * s
                    _li = vector.tensor_scalar(kqt[s][:, 256:512], put[:, 0:256],
                                         lnc[:, b0 + 2:b0 + 3], lnc[:, b0 + 6:b0 + 7],
                                         ALU.subtract, ALU.mult)
                    S["dve"].inc(_li)
                block.vector(ph_ln2_dve)

                def ph_up_te(tensor, s=s):
                    tensor.wait_ge(S["dve"].h, S["dve"].v)
                    tensor.wait_ge(S["act"].h, S["act"].v)
                    tensor.wait_ge(S["w8s"].h, S["w8s"].v)
                    for c in range(2):
                        tensor.transpose(put_bf[:, 512 + 128 * c:512 + 128 * (c + 1)],
                                         kqt[s][:, 256 + 128 * c:256 + 128 * (c + 1)], idb[:])
                    _li = tensor.transpose(put_bf[0:8, 768:896], w8bf[s][:], idb[:])
                    S["te"].inc(_li)
                block.tensor(ph_up_te)

                def ph_up_ev(vector, s=s):
                    vector.wait_ge(S["te"].h, S["te"].v)
                    for c in range(2):
                        vector.tensor_copy(kqt[s][:, 128 * c:128 * (c + 1)],
                                           put_bf[:, 512 + 128 * c:512 + 128 * (c + 1)])
                    _li = vector.tensor_copy(w8tb[:], put_bf[0:8, 768:896])
                    S["evB"].inc(_li)
                block.vector(ph_up_ev)

                def ph_up_mm(tensor, s=s):
                    tensor.wait_ge(S["evB"].h, S["evB"].v)
                    tensor.wait_ge(S["comb"].h, S["comb"].v)
                    for hh, pu in enumerate([pu0, pu1]):
                        for c in range(2):
                            MM(tensor, 
                                pu[:, 0:512], kqt[s][:, 128 * c:128 * (c + 1)],
                                upb[:, 1024 * c + 512 * hh:1024 * c + 512 * (hh + 1)],
                                start=(c == 0), stop=False)
                        MM(tensor, pu[:, 0:512], ones1[:],
                                      rowup[0:1, 512 * hh:512 * (hh + 1)],
                                      start=False, stop=False)
                        _li = MM(tensor, pu[:, 0:512], w8tb[:],
                                      expb[:, 512 * hh:512 * (hh + 1)],
                                      start=False, stop=True)
                    S["phU"].inc(_li)
                block.tensor(ph_up_mm)

                for hh in range(2):
                    def ph_yfin(vector, s=s, hh=hh, buf=buf):
                        vector.wait_ge(S["comb"].h, S["comb"].v)
                        vector.wait_ge(S["phU"].h, S["phU"].v)
                        pu = pu0 if hh == 0 else pu1
                        _li = vector.tensor_add(xb[s][buf][:, 512 * hh:512 * (hh + 1)],
                                          xb[s][buf][:, 512 * hh:512 * (hh + 1)],
                                          pu[:, 0:512])
                        S["comb"].inc(_li)
                    block.vector(ph_yfin)

                def ph_sto(gpsimd, s=s, it=it):
                    gpsimd.wait_ge(S["comb"].h, S["comb"].v)
                    S["sto"].inc(gpsimd.dma_start(
                        y[s, 128 * it:128 * (it + 1), :], xb[s][it % 2][:]), 16)
                block.gpsimd(ph_sto)

            if it + 2 < NT:
                def sec_ldx_late(sync, it=it):
                    sync.wait_ge(S["sto"].h, S["sto"].v)
                    for s in range(2):
                        S["ldx"].inc(sync.dma_start(
                            xb[s][it % 2][:],
                            d["x"][s, 128 * (it + 2):128 * (it + 3), :]), 16)
                block.sync(sec_ldx_late)

        if DBG:
            dbgs = {}
            for nm, t in [("xnb0", xnb[0]), ("htb0", htb[0]), ("xqt0", xqt[0]),
                          ("xkt0", xkt[0]), ("xvt0", xvt[0]), ("sgm0", sgm[0]),
                          ("glog0", glog[0]), ("w80", w8[0]), ("xks", xks),
                          ("gbts", gbts), ("sgs", sgs), ("xhbs", xhbs),
                          ("xhbt0", xhbt[0]), ("w1s", w1s), ("b1c", b1c),
                          ("kqt0", kqt[0]), ("xnf0", xnf[0]), ("s4t", s4t),
                          ("masktok", masktok), ("g2scan", g2scan),
                          ("tsg", tsg), ("gsn", gsn), ("ows0", ows[0]), ("w81", w8[1]), ("glog1", glog[1]),
                          ("gbtt0", gbtt[0])]:
                shp = list(t.shape) if hasattr(t, "shape") else None
                dt_ = {2: BF16}.get(0)
                dbgs[nm] = (nc.dram_tensor("dbg_" + nm, list(t[:].shape),
                                           t[:].dtype, kind="ExternalOutput").ap(), t)

            def sec_dbg(sync):
                sync.wait_ge(S["sto"].h, S["sto"].v)
                for nm, (dr, t) in dbgs.items():
                    S["dbg"].inc(sync.dma_start(dr, t[:]), 16)
                sync.wait_ge(S["dbg"].h, S["dbg"].v)
            block.sync(sec_dbg)

        def sec_fin(sync):
            sync.wait_ge(S["sto"].h, S["sto"].v)
        block.sync(sec_fin)

    return nc


_NC_CACHE = None


def kernel(**inp):
    global _NC_CACHE

    def f32(a):
        return np.ascontiguousarray(np.asarray(a), dtype=np.float32)

    x = f32(inp["L0_attn_x"])
    down_w = f32(inp["L0_attn_down_w"]); down_b = f32(inp["L0_attn_down_b"]).reshape(1, DD)
    up_w = f32(inp["L0_attn_up_w"]); up_b = f32(inp["L0_attn_up_b"]).reshape(1, DM)
    wq = f32(inp["L0_attn_wq"]); wk = f32(inp["L0_attn_wk"])
    wv = f32(inp["L0_attn_wv"]); wo = f32(inp["L0_attn_wo"])
    W1 = f32(inp["L0_attn_W1"])
    b1 = f32(inp["L0_attn_b1"]).reshape(2, 128)
    ttt_ln_w = f32(inp["L0_attn_ttt_ln_w"]).reshape(2, 128)
    ttt_ln_b = f32(inp["L0_attn_ttt_ln_b"]).reshape(2, 128)
    lr_w = f32(inp["L0_attn_lr_w"]).reshape(2, DD)
    lr_b = f32(inp["L0_attn_lr_b"]).reshape(1, 2)
    tok_idx = f32(inp["L0_attn_tok_idx"]).reshape(1, 16)
    post_w = f32(inp["L0_attn_post_w"]).reshape(2, 128)
    post_b = f32(inp["L0_attn_post_b"]).reshape(2, 128)
    ln_w = f32(inp["L0_attn_ln_w"]).reshape(2, 128)
    ln_b = f32(inp["L0_attn_ln_b"]).reshape(2, 128)
    gate_w = f32(inp["L0_ffn_gate_w"]); gate_b = f32(inp["L0_ffn_gate_b"]).reshape(1, 8)
    expert_w = f32(inp["L0_ffn_expert_w"]); expert_b = f32(inp["L0_ffn_expert_b"])

    w1init = np.ascontiguousarray(W1.transpose(1, 0, 2).reshape(128, 256))
    b1init = np.ascontiguousarray(b1[[0, 1, 0, 1], :])
    lr_wT = np.ascontiguousarray(lr_w.T)
    gate_wT = np.ascontiguousarray(gate_w.T)
    ew_t = np.ascontiguousarray(expert_w.transpose(0, 2, 1))
    tokrep = np.ascontiguousarray(np.broadcast_to(tok_idx, (128, 16)))
    ttt_row_w = ttt_ln_w.reshape(1, 256)
    ttt_row_b = ttt_ln_b.reshape(1, 256)
    gw_rep = np.ascontiguousarray(np.broadcast_to(ttt_row_w, (128, 256)))
    b_rep = np.ascontiguousarray(np.broadcast_to(ttt_row_b, (128, 256)))
    gscan0 = np.zeros((128, 128), dtype=np.float32)
    for g in range(4):
        gscan0[32 * g:32 * g + 16, :] = ttt_ln_w[g % 2][None, :]

    import ml_dtypes
    BF = ml_dtypes.bfloat16
    identity_bf = np.eye(128, dtype=BF)
    ones_row_bf = np.ones((1, 128), dtype=BF)
    ind_bf = np.zeros((4, 128), dtype=BF)
    ind2_bf = np.zeros((128, 4), dtype=BF)
    smask = np.zeros((128, 64), dtype=np.float32)
    for g in range(4):
        ind_bf[g, 32 * g:32 * (g + 1)] = 1
        for j in range(16):
            ind2_bf[32 * g + j, g] = 1
            smask[32 * g + j, 16 * g + j:16 * (g + 1)] = 1
    invkrep = np.ascontiguousarray(np.broadcast_to(
        (1.0 / np.arange(1, 17, dtype=np.float32)).reshape(1, 16), (128, 16)))

    shared = dict(
        down_w=down_w, down_b=down_b, up_w=up_w, up_b=up_b,
        wq=wq, wk=wk, wv=wv, wo=wo, w1init=w1init, b1init=b1init,
        lr_wT=lr_wT, lr_b=lr_b, tokrep=tokrep, invkrep=invkrep,
        post_w_c=np.ascontiguousarray(post_w.T), post_b_c=np.ascontiguousarray(post_b.T),
        ln_w_c=np.ascontiguousarray(ln_w.T), ln_b_c=np.ascontiguousarray(ln_b.T),
        gate_wT=gate_wT, gate_b=gate_b, ew_t=ew_t, expert_b=expert_b,
        gw_rep=gw_rep, b_rep=b_rep, gscan0=gscan0,
        identity_bf=identity_bf, ones_row_bf=ones_row_bf,
        ind_bf=ind_bf, ind2_bf=ind2_bf, smask=smask,
    )

    if _NC_CACHE is None:
        _NC_CACHE = build_nc()
    nc = _NC_CACHE

    in_maps = []
    for c in range(8):
        m = dict(shared)
        m["x"] = np.ascontiguousarray(x[2 * c:2 * c + 2])
        in_maps.append(m)

    res = run_bass_kernel_spmd(nc, in_maps, core_ids=list(range(8)))
    out = np.empty((16, SEQ, DM), dtype=np.float32)
    for c in range(8):
        out[2 * c:2 * c + 2] = res.results[c]["y"]
    return out
